# version 23
# baseline (speedup 1.0000x reference)
import numpy as np
import concourse.bass as bass
import concourse.mybir as mybir
from concourse.bass_utils import run_bass_kernel_spmd

F32 = mybir.dt.float32
BF16 = mybir.dt.bfloat16
AF = mybir.ActivationFunctionType
ALU = mybir.AluOpType
AX = mybir.AxisListType

NT = 4096
NO = 2048
D = 1024
NTILE = 32
OTILE = 16
INV75 = float(np.float32(1.0 / 7.5))
INV1024 = float(np.float32(1.0 / 1024.0))


class Builder:
    ENGINES = ("sync", "gpsimd", "vector", "scalar", "tensor")

    def __init__(self, nc, es):
        self.nc = nc
        self.es = es
        self.ops = {e: [] for e in self.ENGINES}
        self.sem = {}
        self.cnt = {}
        for e in ("vector", "scalar", "tensor"):
            self.sem[e] = es.enter_context(nc.semaphore("sem_" + e))
            self.cnt[e] = 0
        self.dsem = {}
        self.dcnt = {}

    def dma_sem(self, name):
        if name not in self.dsem:
            self.dsem[name] = self.es.enter_context(self.nc.semaphore(name))
            self.dcnt[name] = 0
        return self.dsem[name]

    def emit(self, eng, fn, waits=(), inc=False):
        w = [t for t in waits if t is not None]
        tok = None
        sem = None
        if inc:
            self.cnt[eng] += 1
            sem = self.sem[eng]
            tok = (sem, self.cnt[eng])

        def run(engine, w=w, fn=fn, sem=sem):
            for (s, v) in w:
                engine.wait_ge(s, v)
            instr = fn(engine)
            if sem is not None:
                instr.then_inc(sem, 1)
        self.ops[eng].append(run)
        return tok

    def dma(self, queue, stream, out, in_, waits=(), **kw):
        sem = self.dma_sem(stream)
        self.dcnt[stream] += 16
        tok = (sem, self.dcnt[stream])
        w = [t for t in waits if t is not None]

        def run(engine, w=w, out=out, in_=in_, sem=sem, kw=kw):
            for (s, v) in w:
                engine.wait_ge(s, v)
            engine.dma_start(out=out, in_=in_, **kw).then_inc(sem, 16)
        self.ops[queue].append(run)
        return tok

    def wait(self, eng, waits):
        w = [t for t in waits if t is not None]

        def run(engine, w=w):
            for (s, v) in w:
                engine.wait_ge(s, v)
        self.ops[eng].append(run)


def build_program():
    nc = bass.Bass("TRN2", target_bir_lowering=False)
    x_h = nc.dram_tensor("x", [NT, D], F32, kind="ExternalInput").ap()
    wq_h = nc.dram_tensor("wq", [D, D], F32, kind="ExternalInput").ap()
    wk_h = nc.dram_tensor("wk", [D, D], F32, kind="ExternalInput").ap()
    wv_h = nc.dram_tensor("wv", [D, D], F32, kind="ExternalInput").ap()
    wp_h = nc.dram_tensor("wpost", [D, D], F32, kind="ExternalInput").ap()
    wf_h = nc.dram_tensor("w", [D, D], F32, kind="ExternalInput").ap()
    bias_h = nc.dram_tensor("bias", [D], F32, kind="ExternalInput").ap()
    out_h = nc.dram_tensor("out", [NO, D], F32, kind="ExternalOutput").ap()
    xn_d = nc.dram_tensor("xn_d", [NT, D], BF16, kind="Internal").ap()
    kt_d = nc.dram_tensor("kt_d", [D, NT], BF16, kind="Internal").ap()
    v_d = nc.dram_tensor("v_d", [NTILE, 16, 128, 64], BF16, kind="Internal").ap()
    a_d = nc.dram_tensor("a_d", [OTILE, 16, 64, 128], BF16, kind="Internal").ap()
    xd_d = nc.dram_tensor("xd_d", [NO, D], BF16, kind="Internal").ap()
    wd_d = nc.dram_tensor("wd_d", [D, D], BF16, kind="Internal").ap()
    s_dd = nc.dram_tensor("s_dd", [NO, D], F32, kind="Internal").ap()
    z_d = nc.dram_tensor("z_d", [16, NO], F32, kind="Internal").ap()
    zr_d = nc.dram_tensor("zr_d", [16, NO], F32, kind="Internal").ap()

    from contextlib import ExitStack
    es = ExitStack()
    B = Builder(nc, es)

    ABF = es.enter_context(nc.sbuf_tensor("ABF", [128, 74752], BF16))
    AFP = es.enter_context(nc.sbuf_tensor("AFP", [128, 15488], F32))
    STAT = es.enter_context(nc.sbuf_tensor("STAT", [128, 192], F32))
    PSA = es.enter_context(nc.psum_tensor("PSA", [128, 2048], F32))
    PSB = es.enter_context(nc.psum_tensor("PSB", [128, 2048], F32))

    def abf(kb, p, n):
        return ABF[0:p, kb * 512: kb * 512 + n]

    def afp(kb, p, n):
        return AFP[0:p, kb * 256: kb * 256 + n]

    XNT = abf(50, 128, 8 * NT)
    QT = abf(114, 128, 8 * NO)
    WLD = [abf(0, 128, 8192), abf(16, 128, 8192)]
    XNBF = [abf(32, 128, 1024), abf(34, 128, 1024)]
    KST = [abf(36, 128, 512), abf(37, 128, 512), abf(42, 128, 512), abf(43, 128, 512)]
    VST = [abf(38, 128, 512), abf(39, 128, 512), abf(40, 128, 512), abf(41, 128, 512)]
    KP = [abf(0, 128, 4096), abf(8, 128, 4096)]
    VA1 = [abf(16, 128, 2080), abf(21, 128, 2080)]
    VA2 = [abf(26, 128, 2080), abf(31, 128, 2080)]
    EB = [abf(36, 128, 2048), abf(40, 128, 2048)]
    EB4 = [abf(36, 128, 1024), abf(38, 128, 1024), abf(40, 128, 1024), abf(42, 128, 1024)]
    XDBF = abf(48, 128, 1024)
    WPB = abf(50, 128, 8 * D)
    XDT = abf(66, 128, 8 * NO)
    WDT = abf(98, 128, 8 * D)
    ATB = [abf(16, 128, 1024), abf(18, 128, 1024)]
    ATT = [abf(20, 128, 1024), abf(22, 128, 1024)]

    XT = [afp(0, 128, 1024), afp(4, 128, 1024), afp(8, 128, 1024), afp(12, 128, 1024),
          afp(24, 128, 1024), afp(28, 128, 1024), afp(32, 128, 1024), afp(36, 128, 1024)]
    SQ = afp(16, 128, 1024)
    SQ2 = afp(20, 128, 1024)
    U_ = afp(0, 128, 1024)
    AU_ = afp(4, 128, 1024)
    X5 = AU_
    PQ1 = afp(8, 128, 1024)
    PQ2 = afp(12, 128, 1024)
    ISTq = afp(16, 128, 1024)
    STPq = afp(20, 128, 1024)
    TQ = afp(24, 128, 1024)
    RQ = afp(28, 128, 1024)
    XQ = [afp(32, 128, 1024), afp(36, 128, 1024)]
    XNQ = afp(40, 128, 1024)
    XNQR = [afp(40, 128, 1024), afp(48, 128, 1024)]
    ASF2 = ABF[0:64, 22528:23552]
    ZRR = {1: AFP[0:1, 13312:13824], 2: AFP[32:33, 13312:13824]}
    ZR = {1: AFP[0:1, 13824:14336], 2: AFP[32:33, 13824:14336]}
    ONE_A = AFP[0:1, 14336:14464]
    ONE_B = AFP[32:33, 14336:14464]
    ONE_C = AFP[64:65, 14336:14464]
    BIASR = AFP[64:65, 13312:14336]
    ZONE = {1: ONE_A, 2: ONE_B}
    DTp = [afp(0, 128, 1024), afp(4, 128, 1024)]
    GSp = [afp(24, 128, 1024), afp(28, 128, 1024)]
    STLp = [afp(8, 128, 1024), afp(16, 128, 1024)]
    XNQ4 = [afp(40, 128, 1024), afp(12, 128, 1024)]
    XA = [afp(32, 128, 1024), afp(36, 128, 1024)]
    XB = [afp(20, 128, 1024), afp(44, 128, 1024)]
    S1c = [STAT[:, 64 + k:65 + k] for k in range(4)]
    S2c = [STAT[:, 68 + k:69 + k] for k in range(4)]
    MVc = [STAT[:, 72 + k:73 + k] for k in range(4)]
    MSc = [STAT[:, 76 + k:77 + k] for k in range(4)]
    VVc = [STAT[:, 80 + k:81 + k] for k in range(4)]
    QBM = STAT[:, 96:128]
    QSC = STAT[:, 128:160]
    QIS = STAT[:, 160:192]

    with es, nc.Block() as blk:
        def wld_ap(wh):
            return wh.rearrange("(c p) j -> p c j", p=128)
        t_wq = B.dma("gpsimd", "d_wq", WLD[0].rearrange("p (c j) -> p c j", j=1024), wld_ap(wq_h))
        t_wk = B.dma("gpsimd", "d_wk", WLD[1].rearrange("p (c j) -> p c j", j=1024), wld_ap(wk_h))

        stats_tok = {}
        rstd_tok = {}
        mrs_tok = {}
        xn_tok = {}
        store_tok = {}

        s1_free = {}
        s2_free = {}

        def emit_stats(i):
            p = i % 8
            q = i % 4
            ta = B.emit("scalar", lambda s, p=p, q=q: s.activation(
                SQ[:, :], XT[p][:, :], AF.Square, accum_out=S2c[q]),
                waits=[t_ld_tok[i], s2_free.get(i - 4)], inc=True)
            tb = B.emit("scalar", lambda s, p=p, q=q: s.activation(
                SQ2[:, :], XT[p][:, :], AF.Identity, accum_out=S1c[q]),
                waits=[s1_free.get(i - 4)], inc=True)
            tv = B.emit("vector", lambda v, q=q: v.tensor_scalar(MVc[q], S1c[q], INV1024, None, ALU.mult),
                        waits=[tb, mrs_tok.get(i - 4)], inc=True)
            s1_free[i] = tv
            tv = B.emit("vector", lambda v, q=q: v.tensor_tensor(MSc[q], MVc[q], MVc[q], ALU.mult),
                        waits=[tv, mrs_tok.get(i - 4)], inc=True)
            tv = B.emit("vector", lambda v, q=q: v.scalar_tensor_tensor(
                VVc[q], S2c[q], INV1024, MSc[q], ALU.mult, ALU.subtract),
                waits=[tv, rstd_tok.get(i - 4)], inc=True)
            s2_free[i] = tv
            tv = B.emit("vector", lambda v, q=q: v.tensor_scalar(VVc[q], VVc[q], 1e-5, None, ALU.add),
                        waits=[tv], inc=True)
            stats_tok[i] = tv

        def emit_rstd(i):
            q = i % 4
            ta = B.emit("scalar", lambda s, q=q: s.activation(VVc[q], VVc[q], AF.Ln),
                        waits=[stats_tok[i]], inc=True)
            rstd_tok[i] = B.emit("scalar", lambda s, i=i, q=q: s.activation(
                STAT[:, i:i + 1], VVc[q], AF.Exp, scale=-0.5), waits=[ta], inc=True)

        def emit_mrs(i):
            q = i % 4
            mrs_tok[i] = B.emit("vector", lambda v, i=i, q=q: v.scalar_tensor_tensor(
                STAT[:, 32 + i:33 + i], MVc[q], -1.0, STAT[:, i:i + 1], ALU.mult, ALU.mult),
                waits=[rstd_tok[i]], inc=True)

        def emit_xn(i):
            p = i % 8
            rb = STAT[:, i:i + 1].rearrange("p (b o) -> p b o", o=1).broadcast_to([128, 1, 1024])
            mb = STAT[:, 32 + i:33 + i].rearrange("p (b o) -> p b o", o=1).broadcast_to([128, 1, 1024])
            t1 = B.emit("vector", lambda v, p=p, rb=rb: v.tensor_tensor(
                XT[p][:, :].rearrange("p (b w) -> p b w", w=1024),
                XT[p][:, :].rearrange("p (b w) -> p b w", w=1024), rb, ALU.mult),
                waits=[mrs_tok[i]], inc=True)
            t2 = B.emit("vector", lambda v, i=i, p=p, mb=mb: v.tensor_tensor(
                XNBF[i % 2][:, :].rearrange("p (b w) -> p b w", w=1024),
                XT[p][:, :].rearrange("p (b w) -> p b w", w=1024), mb, ALU.add),
                waits=[t1, store_tok.get(i - 2)], inc=True)
            xn_tok[i] = t2
            store_tok[i] = B.dma("gpsimd", "d_xnw%d" % (i % 2), xn_d[i * 128:(i + 1) * 128, :],
                                 XNBF[i % 2][:, :], waits=[t2])

        t_ld_tok = {}
        t_xnt_cb = {}
        pb = [PSA[:, k * 512:(k + 1) * 512] for k in range(4)]
        ev_tok = {}
        mm_tok = {}
        kst_dma = {}
        P2S = {"g": 0, "kn": 0, "q_last": None}

        def emit_q_group(oc, cb):
            g = P2S["g"]; P2S["g"] += 1
            w0 = [t_xnt_cb[cb], t_wq, ev_tok.get(g - 4)]
            tt = None
            for c in range(8):
                tt = B.emit("tensor", lambda t, c=c, g=g, oc=oc, cb=cb:
                            t.matmul(pb[g % 4],
                                     WLD[0][:, c * 1024 + oc * 128: c * 1024 + (oc + 1) * 128],
                                     XNT[:, c * NT + cb * 512: c * NT + cb * 512 + 512],
                                     start=(c == 0), stop=(c == 7)),
                            waits=(w0 if c == 0 else []), inc=(c == 7))
            mm_tok[g] = tt
            P2S["q_last"] = tt
            ev_tok[g] = B.emit("scalar", lambda s, g=g, oc=oc, cb=cb:
                               s.activation(QT[:, oc * NO + cb * 512: oc * NO + cb * 512 + 512], pb[g % 4],
                                            AF.Identity),
                               waits=[tt], inc=True)

        def emit_k_group(oc, cb):
            g = P2S["g"]; P2S["g"] += 1
            kn = P2S["kn"]; P2S["kn"] += 1
            w0 = [t_xnt_cb[cb], t_wk, ev_tok.get(g - 4)]
            tt = None
            for c in range(8):
                tt = B.emit("tensor", lambda t, c=c, g=g, oc=oc, cb=cb:
                            t.matmul(pb[g % 4],
                                     WLD[1][:, c * 1024 + oc * 128: c * 1024 + (oc + 1) * 128],
                                     XNT[:, c * NT + cb * 512: c * NT + cb * 512 + 512],
                                     start=(c == 0), stop=(c == 7)),
                            waits=(w0 if c == 0 else []), inc=(c == 7))
            mm_tok[g] = tt
            j = kn % 4
            ev_tok[g] = B.emit("scalar", lambda s, g=g, j=j: s.activation(KST[j][:, :], pb[g % 4], AF.Identity),
                               waits=[tt, kst_dma.get(kn - 4)], inc=True)
            kst_dma[kn] = B.dma("gpsimd", "d_ks%d" % (kn % 4),
                                kt_d[oc * 128:(oc + 1) * 128, cb * 512:(cb + 1) * 512],
                                KST[j][:, :], waits=[ev_tok[g]])
        for i in range(NTILE + 1):
            if i < NTILE:
                t_ld_tok[i] = B.dma("sync", "d_x%d" % (i % 8),
                                    XT[i % 8][:, :],
                                    x_h[i * 128:(i + 1) * 128, :], waits=[xn_tok.get(i - 8)])
                emit_stats(i)
                emit_rstd(i)
            if i >= 1:
                emit_mrs(i - 1)
                emit_xn(i - 1)
            if i >= 4 and i % 4 == 0:
                cb = i // 4 - 1
                tok = None
                for c in range(8):
                    tok = B.dma("sync", "d_xnt%d" % (cb % 2),
                                XNT[:, c * NT + cb * 512: c * NT + cb * 512 + 512],
                                xn_d[cb * 512:(cb + 1) * 512, c * 128:(c + 1) * 128],
                                waits=[store_tok[4 * cb + 3], store_tok[4 * cb + 2]],
                                transpose=True)
                t_xnt_cb[cb] = tok
                if cb < 4:
                    for oc in range(8):
                        emit_q_group(oc, cb)
                for oc in range(8):
                    emit_k_group(oc, cb)
        t_xnt = t_xnt_cb[7]

        t_wv = B.dma("gpsimd", "d_wv", WLD[0].rearrange("p (c j) -> p c j", j=1024), wld_ap(wv_h),
                     waits=[P2S["q_last"]])
        vst_dma = {}
        v15_mm = None
        for ti in range(32):
            for heb in range(1):
                g = P2S["g"]; P2S["g"] += 1
                vg = ti
                w0 = [t_xnt_cb[ti // 4], t_wv, ev_tok.get(g - 4)]
                tt = None
                for c in range(8):
                    tt = B.emit("tensor", lambda t, c=c, g=g, ti=ti, heb=heb:
                                t.matmul(pb[g % 4],
                                         XNT[:, c * NT + ti * 128: c * NT + (ti + 1) * 128],
                                         WLD[0][:, c * 1024 + heb * 512: c * 1024 + (heb + 1) * 512],
                                         start=(c == 0), stop=(c == 7)),
                                waits=(w0 if c == 0 else []), inc=(c == 7))
                mm_tok[g] = tt
                if ti == 15:
                    v15_mm = tt
                j = vg % 4
                ev_tok[g] = B.emit("scalar", lambda s, g=g, j=j: s.activation(VST[j][:, :], pb[g % 4], AF.Identity),
                                   waits=[tt, vst_dma.get(vg - 4)], inc=True)
                vst_dma[vg] = B.dma("gpsimd", "d_vs%d" % (vg % 4),
                                    v_d[ti, heb * 8:(heb + 1) * 8].rearrange("h p e -> p h e"),
                                    VST[j][:, :].rearrange("p (h e) -> p h e", e=64),
                                    waits=[ev_tok[g]])
        pe_p2_done = mm_tok[P2S["g"] - 1]
        WVX = [ABF[:, 25600 + c * 4096: 25600 + c * 4096 + 512] for c in range(8)]
        t_wvx = None
        for c in range(8):
            t_wvx = B.dma("gpsimd", "d_wv", WVX[c], wv_h[c * 128:(c + 1) * 128, 512:1024],
                          waits=([v15_mm] if c == 0 else []))
        pe_p2_cnt = B.cnt["tensor"]
        p2_stores = [(B.dma_sem(s), B.dcnt[s]) for s in
                     ("d_ks0", "d_ks1", "d_ks2", "d_ks3", "d_vs0", "d_vs1", "d_vs2", "d_vs3")]

        tva = None
        for VA in (VA1[0], VA1[1], VA2[0], VA2[1]):
            tva = B.emit("vector", lambda v, VA=VA: v.memset(
                VA.rearrange("p (c o) -> p c o", o=65)[:, :, 64:65], 1.0), waits=[pe_p2_done], inc=True)
        tva = B.emit("vector", lambda v: v.memset(ONE_A, 1.0), waits=[tva], inc=True)
        tva = B.emit("vector", lambda v: v.memset(ONE_B, 1.0), waits=[tva], inc=True)
        tva = B.emit("vector", lambda v: v.memset(ONE_C, 1.0), waits=[tva], inc=True)
        t_bias = B.dma("sync", "d_misc", BIASR, bias_h.rearrange("(o d) -> o d", o=1), waits=[tva])

        kp_tok = {}
        va_tok = {}
        for hp0 in (0, 1):
            w = p2_stores + [tva]
            kp_tok[hp0] = B.dma("sync", "d_kp%d" % hp0, KP[hp0][:, :], kt_d[hp0 * 128:(hp0 + 1) * 128, :], waits=w)
            va_tok[hp0] = []
            for (vi, (VA, h)) in enumerate(((VA1[hp0], 2 * hp0), (VA2[hp0], 2 * hp0 + 1))):
                va_tok[hp0].append(B.dma("sync", "d_va%d_%d" % (vi, hp0),
                                         VA.rearrange("p (c o) -> p c o", o=65)[:, :, 0:64],
                                         v_d[:, h].rearrange("c p e -> p c e"), waits=w))

        SPp = [PSA[:, 0:1024], PSA[:, 1024:2048], PSB[:, 1024:2048]]
        prev_exp_by_bank = {0: None, 1: None, 2: None}
        ASET = [(PSB[0:65, 0:512], PSB[0:65, 512:1024])]
        pb2 = [PSB[:, 1024:1536], PSB[:, 1536:2048]]
        VSF = [AFP[:, 11264:11776], AFP[:, 11776:12288]]
        vd_state = {"n": 0, "ev": {}, "st": {}, "last_mm": None}
        vd_queue = list(range(32))

        def vd_group(ti):
            g2 = vd_state["n"]
            vd_state["n"] += 1
            w0 = [vd_state["ev"].get(g2 - 2), t_wvx, t_xnt]
            tt = None
            for c in range(8):
                tt = B.emit("tensor", lambda t, c=c, ti=ti, g2=g2: t.matmul(
                    pb2[g2 % 2], XNT[:, c * NT + ti * 128: c * NT + (ti + 1) * 128],
                    WVX[c], start=(c == 0), stop=(c == 7)),
                    waits=(w0 if c == 0 else []), inc=(c == 7))
            vd_state["last_mm"] = tt
            ev = B.emit("vector", lambda v, g2=g2: v.tensor_copy(VSF[g2 % 2], pb2[g2 % 2]),
                        waits=[tt, vd_state["st"].get(g2 - 2)], inc=True)
            vd_state["ev"][g2] = ev
            vd_state["st"][g2] = B.dma("gpsimd", "d_vs%d" % (g2 % 2),
                                       v_d[ti, 8:16].rearrange("h p e -> p h e"),
                                       VSF[g2 % 2].rearrange("p (h e) -> p h e", e=64),
                                       waits=[ev])
        ZBV = [(PSB[0:64, 0:512], PSB[0:64, 512:1024]),
               (PSB[0:64, 1024:1536], PSB[0:64, 1536:2048])]
        pe_hp_last = {}
        prev_av_for_E = [None, None, None, None]
        prev_set_tail = [None, None]
        z_prev = {1: None, 2: None}
        src_free = {}
        chainsrc = {}
        xnq_tok = {}
        chain_state = {"last": None, "prev_store": None}
        xd_store = []
        wd_store = []
        az_prev = {}
        qload_tok = {}

        def quant_chain_steps(src_ap, dst_dram, start_wait):
            toks = {}
            steps = []
            st = {"tv": None}
            w_first = [start_wait, chain_state["last"]]
            if chain_state["last"] is None:
                w_first.append(xn_tok[31])
            ps = chain_state["prev_store"]
            iscb = QIS.rearrange("p (b o) -> p b o", o=1).broadcast_to([128, 32, 32])
            scb = QSC.rearrange("p (b o) -> p b o", o=1).broadcast_to([128, 32, 32])

            def op(build, key=None):
                def run(build=build, key=key):
                    w = w_first if st["tv"] is None else [st["tv"]]
                    st["tv"] = B.emit("vector", build, waits=w, inc=True)
                    if key:
                        toks[key] = st["tv"]
                steps.append(run)

            op(lambda v, src_ap=src_ap: v.tensor_reduce(
                QBM, src_ap.rearrange("p (b w) -> p b w", w=32), AX.X, ALU.max,
                apply_absolute_value=True))
            op(lambda v: v.tensor_scalar(QSC, QBM, INV75, 1e-30, ALU.mult, ALU.max))
            op(lambda v: v.reciprocal(QIS, QSC))
            op(lambda v, src_ap=src_ap, iscb=iscb: v.tensor_tensor(
                U_.rearrange("p (b w) -> p b w", w=32),
                src_ap.rearrange("p (b w) -> p b w", w=32), iscb, ALU.mult), key="src_free")
            op(lambda v: v.tensor_scalar(U_, U_, 7.5, -7.5, ALU.min, ALU.max))
            op(lambda v: v.tensor_scalar(PQ1, U_, -1.0, None, ALU.mult))
            op(lambda v: v.tensor_tensor(AU_, U_, PQ1, ALU.max))
            op(lambda v: v.tensor_scalar(PQ1, AU_, 2.0, -4.0, ALU.is_ge, ALU.mult))
            op(lambda v: v.tensor_scalar(PQ2, AU_, 4.0, -2.0, ALU.is_ge, ALU.mult))
            op(lambda v: v.scalar_tensor_tensor(ISTq, PQ1, 8.0, PQ2, ALU.add, ALU.add))
            op(lambda v: v.tensor_scalar(PQ1, AU_, 2.0, 0.125, ALU.is_ge, ALU.mult))
            op(lambda v: v.tensor_scalar(PQ2, AU_, 4.0, 0.25, ALU.is_ge, ALU.mult))
            op(lambda v: v.scalar_tensor_tensor(STPq, PQ1, 0.125, PQ2, ALU.add, ALU.add))
            op(lambda v: v.tensor_tensor(TQ, AU_, ISTq, ALU.mult))
            op(lambda v: v.tensor_scalar(X5, TQ, 0.5, None, ALU.add))
            for wgt in (16.0, 8.0, 4.0, 2.0, 1.0):
                op(lambda v, wgt=wgt: v.tensor_scalar(PQ1, X5, wgt, -wgt, ALU.is_ge, ALU.mult))
                op(lambda v: v.tensor_tensor(X5, X5, PQ1, ALU.add))
            op(lambda v: v.tensor_scalar(RQ, TQ, 0.5, None, ALU.add))
            op(lambda v: v.tensor_tensor(RQ, RQ, X5, ALU.subtract))
            op(lambda v: v.tensor_tensor(RQ, RQ, STPq, ALU.mult))
            op(lambda v: v.tensor_scalar(PQ1, U_, 0.0, 2.0, ALU.is_ge, ALU.mult))
            op(lambda v: v.tensor_tensor(PQ2, RQ, PQ1, ALU.mult))
            op(lambda v: v.tensor_tensor(RQ, PQ2, RQ, ALU.subtract))

            def run_final():
                tv = B.emit("vector", lambda v, scb=scb: v.tensor_tensor(
                    XDBF.rearrange("p (b w) -> p b w", w=32),
                    RQ.rearrange("p (b w) -> p b w", w=32), scb, ALU.mult),
                    waits=[st["tv"], ps], inc=True)
                toks["last"] = tv
                chain_state["last"] = tv
                toks["store"] = B.dma("gpsimd", "d_xd", dst_dram, XDBF, waits=[tv])
                chain_state["prev_store"] = toks["store"]
            steps.append(run_final)
            return toks, steps

        def task_load(u):
            if u < 16:
                qload_tok[u] = B.dma("sync", "d_xq%d" % (u % 2), XQ[u % 2][:, :],
                                     x_h[u * 128:(u + 1) * 128, :],
                                     waits=[src_free.get(u - 2), xn_tok[31] if u <= 1 else None])
            elif u < 24:
                j = u - 16
                qload_tok[u] = B.dma("sync", "d_xq%d" % (u % 2), XQ[u % 2][:, :],
                                     wf_h[j * 128:(j + 1) * 128, :], waits=[src_free.get(u - 2)])

        task_load(0)
        for u in range(32):
            hp, qgi = divmod(u, 4)
            h1, h2 = 2 * hp, 2 * hp + 1
            bufK = hp % 2
            aset = 0
            A1, A2 = ASET[aset]
            if qgi == 0 and 1 <= hp <= 6:
                nb = (hp + 1) % 2
                w = [pe_hp_last[hp - 1]]
                kp_tok[hp + 1] = B.dma("sync", "d_kp%d" % nb, KP[nb][:, :],
                                       kt_d[(hp + 1) * 128:(hp + 2) * 128, :], waits=w)
                va_tok[hp + 1] = []
                wv2 = list(w)
                if hp + 1 >= 4:
                    wv2 += [vd_state["st"][31], vd_state["st"][30]]
                for (vi, (VA, h)) in enumerate(((VA1[nb], 2 * (hp + 1)), (VA2[nb], 2 * (hp + 1) + 1))):
                    va_tok[hp + 1].append(B.dma("sync", "d_va%d_%d" % (vi, nb),
                                                VA.rearrange("p (c o) -> p c o", o=65)[:, :, 0:64],
                                                v_d[:, h].rearrange("c p e -> p c e"), waits=wv2))
            unit_ready = [kp_tok[hp]] + va_tok[hp]
            task_load(u + 1)

            if u < 16:
                t_xnq = B.emit("scalar", lambda s, u=u: s.activation(
                    XNQR[u % 2][:, :], XQ[u % 2][:, :], AF.Identity,
                    scale=STAT[:, u:u + 1], bias=STAT[:, 32 + u:33 + u]),
                    waits=[qload_tok[u], chainsrc.get(u - 2)], inc=True)
                xnq_tok[u] = t_xnq
            chain_steps = None
            ctoks = None
            if u < 16:
                ctoks, chain_steps = quant_chain_steps(
                    XNQR[u % 2][:, :], xd_d[u * 128:(u + 1) * 128, :], xnq_tok[u])
                src_free[u] = xnq_tok[u]
            elif u < 24:
                j2 = u - 16
                ctoks, chain_steps = quant_chain_steps(
                    XQ[u % 2][:, :], wd_d[j2 * 128:(j2 + 1) * 128, :], qload_tok[u])
            if u == 13:
                p3_wp = B.dma("gpsimd", "d_wp", WPB.rearrange("p (c j) -> p c j", j=1024),
                              wld_ap(wp_h), waits=[vd_state["last_mm"]])
            if u == 17:
                p3_xdt = None
                for ic in range(8):
                    p3_xdt = B.dma("sync", "d_xdt2", XDT[:, ic * NO:(ic + 1) * NO],
                                   xd_d[:, ic * 128:(ic + 1) * 128],
                                   waits=[xd_store[-1], vd_state["last_mm"] if ic == 0 else None],
                                   transpose=True)
            if u == 25:
                p3_wdt = None
                for ic in range(8):
                    p3_wdt = B.dma("sync", "d_wdt2", WDT[:, ic * D:(ic + 1) * D],
                                   wd_d[:, ic * 128:(ic + 1) * 128],
                                   waits=[wd_store[-1], vd_state["last_mm"] if ic == 0 else None],
                                   transpose=True)

            qcol = hp * NO + qgi * 512
            exp_tok = {}
            sc_tok = {}
            av_list = {}

            def do_av(t2, A1=A1, A2=A2, bufK=bufK, aset=aset):
                wav = [exp_tok[t2]]
                if t2 == 0:
                    wav.append(prev_set_tail[aset])
                B.emit("tensor", lambda t, t2=t2, A1=A1, bufK=bufK: t.matmul(
                    A1, VA1[bufK][:, t2 * 65:t2 * 65 + 65],
                    EB4[t2 % 4][:, 0:512],
                    start=(t2 == 0), stop=(t2 == 31)), waits=wav, inc=False)
                av_list[t2] = B.emit("tensor", lambda t, t2=t2, A2=A2, bufK=bufK: t.matmul(
                    A2, VA2[bufK][:, t2 * 65:t2 * 65 + 65],
                    EB4[t2 % 4][:, 512:1024],
                    start=(t2 == 0), stop=(t2 == 31)), waits=[], inc=True)

            nb = 3 if u >= 13 else 2
            for tc in range(32):
                b = tc % nb
                if tc < nb:
                    w_sc = [prev_exp_by_bank[b]]
                    if tc == 0:
                        w_sc += unit_ready
                    if u == 13 and b == 2:
                        w_sc.append(vd_state["ev"][31])
                else:
                    w_sc = [exp_tok[tc - nb]]
                B.emit("tensor", lambda t, tc=tc, b=b, bufK=bufK, qcol=qcol: t.matmul(
                    SPp[b][:, 0:512],
                    KP[bufK][0:64, tc * 128:(tc + 1) * 128],
                    QT[0:64, qcol:qcol + 512],
                    start=True, stop=True, tile_position=(0, 0)), waits=w_sc, inc=False)
                sc_tok[tc] = B.emit("tensor", lambda t, tc=tc, b=b, bufK=bufK, qcol=qcol: t.matmul(
                    SPp[b][:, 512:1024],
                    KP[bufK][64:128, tc * 128:(tc + 1) * 128],
                    QT[64:128, qcol:qcol + 512],
                    start=True, stop=True, tile_position=(64, 0)), waits=[], inc=True)
                we = [sc_tok[tc]]
                if tc >= 4:
                    we.append(av_list[tc - 4])
                else:
                    we.append(prev_av_for_E[tc % 4])
                exp_tok[tc] = B.emit("scalar", lambda s, b=b, tc=tc: s.activation(
                    EB4[tc % 4][:, 0:1024], SPp[b][:, :], AF.Exp, scale=0.125), waits=we, inc=True)
                if tc >= 2:
                    do_av(tc - 2)
                if u <= 11 and tc in (6, 16, 26) and vd_queue:
                    vd_group(vd_queue.pop(0))
                if chain_steps and tc < 27:
                    chain_steps.pop(0)()
            do_av(30)
            do_av(31)
            for tc in range(32 - nb, 32):
                prev_exp_by_bank[tc % nb] = exp_tok[tc]
            prev_av_for_E = [av_list[28], av_list[29], av_list[30], av_list[31]]
            if qgi == 3:
                pe_hp_last[hp] = av_list[31]

            tva_ = B.emit("vector", lambda v: v.tensor_copy(ASF2[:, :], PSB[0:64, 0:1024]),
                          waits=[av_list[31], az_prev.get(1), az_prev.get(2)], inc=True)
            tz = {}
            for hs, Abank, hh in ((1, A1, h1), (2, A2, h2)):
                tz[hs] = B.emit("vector", lambda v, hs=hs, Abank=Abank: v.tensor_copy(
                    ZR[hs], Abank[64:65, :]), waits=[tva_ if hs == 1 else tz[1], z_prev.get(hs)], inc=True)
                az_prev[hs] = B.dma("gpsimd", "d_az%d" % hs,
                                    a_d[qgi * 4:(qgi + 1) * 4, hh].rearrange("b p j -> p b j"),
                                    ASF2[:, (hs - 1) * 512:hs * 512].rearrange("p (b j) -> p b j", j=128),
                                    waits=[tva_])
                z_prev[hs] = B.dma("gpsimd", "d_z%d" % hs,
                                   z_d[hh:hh + 1, qgi * 512:(qgi + 1) * 512], ZR[hs], waits=[tz[hs]])
            prev_set_tail[aset] = tz[2]
            while chain_steps:
                chain_steps.pop(0)()

            if u < 16:
                chainsrc[u] = ctoks["src_free"]
                xd_store.append(ctoks["store"])
            elif u < 24:
                src_free[u] = ctoks["src_free"]
                wd_store.append(ctoks["store"])
        last_chain = chain_state["last"]

        t_wp = p3_wp
        ZV = afp(24, 128, 256)
        ZRV = afp(25, 128, 256)
        t_zv = B.dma("sync", "d_zv", ZV, z_d.rearrange("h (a q) -> (h a) q", a=8),
                     waits=[z_prev[1], z_prev[2], last_chain])
        t_zr = B.emit("vector", lambda v: v.reciprocal(ZRV, ZV), waits=[t_zv], inc=True)
        t_zrs = B.dma("gpsimd", "d_zr", zr_d.rearrange("h (a q) -> (h a) q", a=8), ZRV, waits=[t_zr])
        t_xdt = p3_xdt
        t_wdt = p3_wdt

        PPq = [[PSA[:, 0:512], PSA[:, 512:1024]], [PSA[:, 1024:1536], PSA[:, 1536:2048]]]
        PFq = PSB[:, 0:1024]
        DP = PSB[:, 1024:2048]
        ZPK0 = [AFP[0:1, 13312:14336], AFP[0:1, 14464:15488]]
        ZPK1 = [AFP[64:65, 12288:13312], AFP[64:65, 14464:15488]]
        dmm_tok = {}
        dmul_tok = {}
        xn4_tok = {}
        pmm_tok = {}
        sadd_tok = {}
        gelu_tok = {}
        ffn_tok = {}
        silu_tok = {}
        sg_tok = {}
        outtok = {}
        out_store = {}
        xb_tok = {}

        def p4_stage1(qt):
            pr = qt % 2
            w_z = [dmm_tok.get(qt - 2),
                   t_zrs if qt <= 1 else None,
                   az_prev[1] if qt <= 1 else None,
                   az_prev[2] if qt <= 1 else None]
            tz = None
            for pg in range(2):
                dst = (ZPK0 if pg == 0 else ZPK1)[pr].rearrange("o (c j) -> o c j", j=128)
                srcz = zr_d[pg::2, qt * 128:(qt + 1) * 128].rearrange("(o c) j -> o c j", o=1)
                tz = B.dma("sync", "d_dt%d" % pr, dst, srcz, waits=(w_z if pg == 0 else []))
            t_x4 = B.dma("sync", "d_p4x%d" % pr, XA[pr][:, :], x_h[qt * 128:(qt + 1) * 128, :],
                         waits=[xn4_tok.get(qt - 2),
                                last_chain if qt <= 1 else None,
                                xnq_tok[15] if qt <= 1 else None])
            w_at = [dmul_tok.get(qt - 2), t_wp,
                    t_zrs if qt <= 1 else None,
                    wd_store[-1] if qt == 0 else None,
                    az_prev[1] if qt == 0 else None,
                    az_prev[2] if qt == 0 else None,
                    pe_hp_last[7] if qt == 0 else None]
            t_at_e = B.dma("sync", "d_p4a%d" % pr,
                           ATB[pr][0:64, :].rearrange("p (c j) -> p c j", j=128),
                           a_d[qt].rearrange("(c t) p j -> t c p j", t=2)[0].rearrange("c p j -> p c j"),
                           waits=w_at)
            t_at_o = B.dma("sync", "d_p4a%d" % pr,
                           ATB[pr][64:128, :].rearrange("p (c j) -> p c j", j=128),
                           a_d[qt].rearrange("(c t) p j -> t c p j", t=2)[1].rearrange("c p j -> p c j"),
                           waits=[])
            if qt >= 1:
                pr2 = (qt - 1) % 2
                xb_tok[qt - 1] = B.dma("sync", "d_p4b%d" % pr2, XB[pr2][:, :],
                                       x_h[(qt - 1) * 128:qt * 128, :],
                                       waits=[outtok.get(qt - 3),
                                              last_chain if qt - 1 <= 1 else None])
            w_dm = [tz, dmul_tok.get(qt - 1),
                    prev_exp_by_bank[2] if qt == 0 else None]
            tmm = None
            for pg in range(2):
                ones = ONE_A if pg == 0 else ONE_C
                park = (ZPK0 if pg == 0 else ZPK1)[pr]
                for nh in range(2):
                    tmm = B.emit("tensor", lambda t, pg=pg, nh=nh, ones=ones, park=park:
                                 t.matmul(DP[pg * 64:(pg + 1) * 64, nh * 512:(nh + 1) * 512],
                                          ones[:, 0:64], park[:, nh * 512:(nh + 1) * 512],
                                          start=True, stop=True),
                                 waits=(w_dm if (pg == 0 and nh == 0) else []),
                                 inc=(pg == 1 and nh == 1))
            dmm_tok[qt] = tmm
            t_xn4 = B.emit("scalar", lambda s, qt=qt, pr=pr: s.activation(
                XNQ4[qt % 2][:, :], XA[pr][:, :], AF.Identity,
                scale=STAT[:, qt:qt + 1], bias=STAT[:, 32 + qt:33 + qt]),
                waits=[t_x4, sadd_tok.get(qt - 2),
                       az_prev[1] if qt <= 1 else None,
                       az_prev[2] if qt <= 1 else None], inc=True)
            xn4_tok[qt] = t_xn4
            t_dm = B.emit("vector", lambda v, pr=pr: v.tensor_tensor(
                DTp[pr][:, :], ATB[pr][:, :], DP[:, :], ALU.mult),
                waits=[t_at_o, dmm_tok[qt], gelu_tok.get(qt - 2)], inc=True)
            dmul_tok[qt] = t_dm

        def p4_stage2(qt):
            pr = qt % 2
            t_gelu = B.emit("scalar", lambda s, qt=qt, pr=pr: s.activation(
                ATT[pr][:, :], DTp[pr][:, :], AF.Gelu),
                waits=[dmul_tok[qt], pmm_tok.get(qt - 2)], inc=True)
            gelu_tok[qt] = t_gelu
            ts_list = []
            for half in range(2):
                w0 = [t_gelu, sadd_tok.get(qt - 2)]
                tt = None
                for hc in range(8):
                    tt = B.emit("tensor", lambda t, hc=hc, half=half, qt=qt, pr=pr:
                                t.matmul(PPq[pr][half], ATT[pr][:, hc * 128:(hc + 1) * 128],
                                         WPB[:, hc * 1024 + half * 512: hc * 1024 + half * 512 + 512],
                                         start=(hc == 0), stop=(hc == 7)),
                                waits=(w0 if hc == 0 else []), inc=(hc == 7))
                if half == 1:
                    pmm_tok[qt] = tt
                ts_ = B.emit("vector", lambda v, half=half, pr=pr, qt=qt: v.tensor_tensor(
                    STLp[pr][:, half * 512:(half + 1) * 512], PPq[pr][half],
                    XNQ4[qt % 2][:, half * 512:(half + 1) * 512],
                    ALU.add), waits=[tt, xn4_tok[qt], sg_tok.get(qt - 2)], inc=True)
                ts_list.append(ts_)
            sadd_tok[qt] = ts_list[-1]

        def emit_ffn(qt):
            w0 = [t_xdt, t_wdt, silu_tok.get(qt - 1)]
            if qt == 0:
                w0.append(prev_set_tail[0])
                w0.append(t_bias)
            tt = None
            for half in range(2):
                for ic in range(8):
                    tt = B.emit("tensor", lambda t, ic=ic, half=half, qt=qt:
                                t.matmul(PFq[:, half * 512:(half + 1) * 512],
                                         XDT[:, ic * NO + qt * 128: ic * NO + (qt + 1) * 128],
                                         WDT[:, ic * 1024 + half * 512: ic * 1024 + half * 512 + 512],
                                         start=(ic == 0), stop=False),
                                waits=(w0 if (half == 0 and ic == 0) else []), inc=False)
                tt = B.emit("tensor", lambda t, half=half:
                            t.matmul(PFq[:, half * 512:(half + 1) * 512],
                                     ONE_C[:, :], BIASR[:, half * 512:half * 512 + 512],
                                     start=False, stop=True),
                            waits=[], inc=(half == 1))
            ffn_tok[qt] = tt

        def p4_stage3(qt):
            pr = qt % 2
            t_silu = B.emit("scalar", lambda s, pr=pr: s.activation(GSp[pr][:, :], PFq[:, :], AF.Silu),
                            waits=[ffn_tok[qt], out_store.get(qt - 2)], inc=True)
            silu_tok[qt] = t_silu
            if qt + 1 < OTILE:
                emit_ffn(qt + 1)
            tv = B.emit("vector", lambda v, pr=pr: v.tensor_tensor(
                GSp[pr][:, :], GSp[pr][:, :], STLp[pr][:, :], ALU.mult),
                waits=[t_silu, sadd_tok[qt]], inc=True)
            sg_tok[qt] = tv
            tv = B.emit("vector", lambda v, qt=qt, pr=pr: v.tensor_tensor(
                GSp[pr][:, :], GSp[pr][:, :], XB[pr][:, :], ALU.add),
                waits=[tv, xb_tok[qt]], inc=True)
            outtok[qt] = tv
            out_store[qt] = B.dma("gpsimd", "d_out%d" % pr, out_h[qt * 128:(qt + 1) * 128, :],
                                  GSp[pr][:, :], waits=[tv])

        p4_stage1(0)
        p4_stage1(1)
        emit_ffn(0)
        for qt in range(OTILE):
            p4_stage2(qt)
            if qt >= 1:
                p4_stage3(qt - 1)
            if qt + 2 < OTILE:
                p4_stage1(qt + 2)
        qlast = OTILE - 1
        xb_tok[qlast] = B.dma("sync", "d_p4b%d" % (qlast % 2), XB[qlast % 2][:, :],
                              x_h[qlast * 128:(qlast + 1) * 128, :],
                              waits=[outtok.get(qlast - 2)])
        p4_stage3(qlast)
        B.wait("gpsimd", [out_store[OTILE - 1], out_store[OTILE - 2]])
        B.wait("sync", [out_store[OTILE - 1], out_store[OTILE - 2]])

        @blk.sync
        def _(sync):
            for fn in B.ops["sync"]:
                fn(sync)

        @blk.gpsimd
        def _(g):
            for fn in B.ops["gpsimd"]:
                fn(g)

        @blk.vector
        def _(v):
            for fn in B.ops["vector"]:
                fn(v)

        @blk.scalar
        def _(s):
            for fn in B.ops["scalar"]:
                fn(s)

        @blk.tensor
        def _(t):
            for fn in B.ops["tensor"]:
                fn(t)

    return nc


_NC_CACHE = []


def kernel(L0_attn_X, L0_attn_Wq, L0_attn_Wk, L0_attn_Wv, L0_attn_Wpost,
           L0_ffn_weight, L0_ffn_bias):
    x = np.ascontiguousarray(np.asarray(L0_attn_X, dtype=np.float32))
    wq = np.ascontiguousarray(np.asarray(L0_attn_Wq, dtype=np.float32))
    wk = np.ascontiguousarray(np.asarray(L0_attn_Wk, dtype=np.float32))
    wv = np.ascontiguousarray(np.asarray(L0_attn_Wv, dtype=np.float32))
    wp = np.ascontiguousarray(np.asarray(L0_attn_Wpost, dtype=np.float32))
    wf = np.ascontiguousarray(np.asarray(L0_ffn_weight, dtype=np.float32))
    bias = np.ascontiguousarray(np.asarray(L0_ffn_bias, dtype=np.float32))

    if not _NC_CACHE:
        _NC_CACHE.append(build_program())
    nc = _NC_CACHE[0]

    in_maps = []
    for c in range(8):
        b, h = divmod(c, 2)
        own = x[b, h * 2048:(h + 1) * 2048]
        oth = x[b, (1 - h) * 2048:(2 - h) * 2048]
        xc = np.ascontiguousarray(np.concatenate([own, oth], axis=0))
        in_maps.append({"x": xc, "wq": wq, "wk": wk, "wv": wv,
                        "wpost": wp, "w": wf, "bias": bias})
    res = run_bass_kernel_spmd(nc, in_maps, core_ids=list(range(8)))
    out = np.empty((4, 4096, 1024), dtype=np.float32)
    for c in range(8):
        b, h = divmod(c, 2)
        out[b, h * 2048:(h + 1) * 2048] = res.results[c]["out"]
    return out


# version 24
# speedup vs baseline: 1.0732x; 1.0732x over previous
import numpy as np
import concourse.bass as bass
import concourse.mybir as mybir
from concourse.bass_utils import run_bass_kernel_spmd

F32 = mybir.dt.float32
BF16 = mybir.dt.bfloat16
AF = mybir.ActivationFunctionType
ALU = mybir.AluOpType
AX = mybir.AxisListType

NT = 4096
NO = 2048
D = 1024
NTILE = 32
OTILE = 16
INV75 = float(np.float32(1.0 / 7.5))
INV1024 = float(np.float32(1.0 / 1024.0))


class Builder:
    ENGINES = ("sync", "gpsimd", "vector", "scalar", "tensor")

    def __init__(self, nc, es):
        self.nc = nc
        self.es = es
        self.ops = {e: [] for e in self.ENGINES}
        self.sem = {}
        self.cnt = {}
        for e in ("vector", "scalar", "tensor"):
            self.sem[e] = es.enter_context(nc.semaphore("sem_" + e))
            self.cnt[e] = 0
        self.dsem = {}
        self.dcnt = {}

    def dma_sem(self, name):
        if name not in self.dsem:
            self.dsem[name] = self.es.enter_context(self.nc.semaphore(name))
            self.dcnt[name] = 0
        return self.dsem[name]

    def emit(self, eng, fn, waits=(), inc=False):
        w = [t for t in waits if t is not None]
        tok = None
        sem = None
        if inc:
            self.cnt[eng] += 1
            sem = self.sem[eng]
            tok = (sem, self.cnt[eng])

        def run(engine, w=w, fn=fn, sem=sem):
            for (s, v) in w:
                engine.wait_ge(s, v)
            instr = fn(engine)
            if sem is not None:
                instr.then_inc(sem, 1)
        self.ops[eng].append(run)
        return tok

    def dma(self, queue, stream, out, in_, waits=(), **kw):
        sem = self.dma_sem(stream)
        self.dcnt[stream] += 16
        tok = (sem, self.dcnt[stream])
        w = [t for t in waits if t is not None]

        def run(engine, w=w, out=out, in_=in_, sem=sem, kw=kw):
            for (s, v) in w:
                engine.wait_ge(s, v)
            engine.dma_start(out=out, in_=in_, **kw).then_inc(sem, 16)
        self.ops[queue].append(run)
        return tok

    def wait(self, eng, waits):
        w = [t for t in waits if t is not None]

        def run(engine, w=w):
            for (s, v) in w:
                engine.wait_ge(s, v)
        self.ops[eng].append(run)


def build_program():
    nc = bass.Bass("TRN2", target_bir_lowering=False)
    x_h = nc.dram_tensor("x", [NT, D], F32, kind="ExternalInput").ap()
    wq_h = nc.dram_tensor("wq", [D, D], F32, kind="ExternalInput").ap()
    wk_h = nc.dram_tensor("wk", [D, D], F32, kind="ExternalInput").ap()
    wv_h = nc.dram_tensor("wv", [D, D], F32, kind="ExternalInput").ap()
    wp_h = nc.dram_tensor("wpost", [D, D], F32, kind="ExternalInput").ap()
    wf_h = nc.dram_tensor("w", [D, D], F32, kind="ExternalInput").ap()
    bias_h = nc.dram_tensor("bias", [D], F32, kind="ExternalInput").ap()
    out_h = nc.dram_tensor("out", [NO, D], F32, kind="ExternalOutput").ap()
    xn_d = nc.dram_tensor("xn_d", [NT, D], BF16, kind="Internal").ap()
    kt_d = nc.dram_tensor("kt_d", [D, NT], BF16, kind="Internal").ap()
    v_d = nc.dram_tensor("v_d", [NTILE, 16, 128, 64], BF16, kind="Internal").ap()
    a_d = nc.dram_tensor("a_d", [OTILE, 16, 64, 128], BF16, kind="Internal").ap()
    xd_d = nc.dram_tensor("xd_d", [NO, D], BF16, kind="Internal").ap()
    wd_d = nc.dram_tensor("wd_d", [D, D], BF16, kind="Internal").ap()
    s_dd = nc.dram_tensor("s_dd", [NO, D], F32, kind="Internal").ap()
    z_d = nc.dram_tensor("z_d", [16, NO], F32, kind="Internal").ap()
    zr_d = nc.dram_tensor("zr_d", [16, NO], F32, kind="Internal").ap()

    from contextlib import ExitStack
    es = ExitStack()
    B = Builder(nc, es)

    ABF = es.enter_context(nc.sbuf_tensor("ABF", [128, 74752], BF16))
    AFP = es.enter_context(nc.sbuf_tensor("AFP", [128, 15488], F32))
    STAT = es.enter_context(nc.sbuf_tensor("STAT", [128, 176], F32))
    PSA = es.enter_context(nc.psum_tensor("PSA", [128, 2048], F32))
    PSB = es.enter_context(nc.psum_tensor("PSB", [128, 2048], F32))

    def abf(kb, p, n):
        return ABF[0:p, kb * 512: kb * 512 + n]

    def afp(kb, p, n):
        return AFP[0:p, kb * 256: kb * 256 + n]

    XNT = abf(50, 128, 8 * NT)
    QT = abf(114, 128, 8 * NO)
    WLD = [abf(0, 128, 8192), abf(16, 128, 8192)]
    XNBF = [abf(32, 128, 1024), abf(34, 128, 1024)]
    KST = [abf(36, 128, 512), abf(37, 128, 512), abf(42, 128, 512), abf(43, 128, 512)]
    VST = [abf(38, 128, 512), abf(39, 128, 512), abf(40, 128, 512), abf(41, 128, 512)]
    KP = [abf(0, 128, 4096), abf(8, 128, 4096)]
    VA1 = [abf(16, 128, 2080), abf(21, 128, 2080)]
    VA2 = [abf(26, 128, 2080), abf(31, 128, 2080)]
    EB = [abf(36, 128, 2048), abf(40, 128, 2048)]
    EB4 = [abf(36, 128, 1024), abf(38, 128, 1024), abf(40, 128, 1024), abf(42, 128, 1024)]
    XDBF = abf(48, 128, 1024)
    WPB = abf(50, 128, 8 * D)
    XDT = abf(66, 128, 8 * NO)
    WDT = abf(98, 128, 8 * D)
    ATB = [abf(16, 128, 1024), abf(18, 128, 1024)]
    ATT = [abf(20, 128, 1024), abf(22, 128, 1024)]

    XT = [afp(0, 128, 1024), afp(4, 128, 1024), afp(8, 128, 1024), afp(12, 128, 1024)]
    SQ = afp(16, 128, 1024)
    SQ2 = afp(20, 128, 1024)
    U_ = afp(0, 128, 1024)
    AU_ = afp(4, 128, 1024)
    X5 = AU_
    PQ1 = afp(8, 128, 1024)
    PQ2 = afp(12, 128, 1024)
    ISTq = afp(16, 128, 1024)
    STPq = afp(20, 128, 1024)
    TQ = afp(24, 128, 1024)
    RQ = afp(28, 128, 1024)
    XQ = [afp(32, 128, 1024), afp(36, 128, 1024)]
    XNQ = afp(40, 128, 1024)
    XNQR = [afp(40, 128, 1024), afp(48, 128, 1024)]
    ASF2 = ABF[0:64, 22528:23552]
    ZRR = {1: AFP[0:1, 13312:13824], 2: AFP[32:33, 13312:13824]}
    ZR = {1: AFP[0:1, 13824:14336], 2: AFP[32:33, 13824:14336]}
    ONE_A = AFP[0:1, 14336:14464]
    ONE_B = AFP[32:33, 14336:14464]
    ONE_C = AFP[64:65, 14336:14464]
    BIASR = AFP[64:65, 13312:14336]
    ZONE = {1: ONE_A, 2: ONE_B}
    DTp = [afp(0, 128, 1024), afp(4, 128, 1024)]
    GSp = [afp(24, 128, 1024), afp(28, 128, 1024)]
    STLp = [afp(8, 128, 1024), afp(16, 128, 1024)]
    XNQ4 = [afp(40, 128, 1024), afp(12, 128, 1024)]
    XA = [afp(32, 128, 1024), afp(36, 128, 1024)]
    XB = [afp(20, 128, 1024), afp(44, 128, 1024)]
    S1c = [STAT[:, 64:65], STAT[:, 70:71]]
    S2c = [STAT[:, 65:66], STAT[:, 71:72]]
    MVc = [STAT[:, 66:67], STAT[:, 72:73]]
    MSc = [STAT[:, 67:68], STAT[:, 73:74]]
    VVc = [STAT[:, 68:69], STAT[:, 74:75]]
    QBM = STAT[:, 80:112]
    QSC = STAT[:, 112:144]
    QIS = STAT[:, 144:176]

    with es, nc.Block() as blk:
        def wld_ap(wh):
            return wh.rearrange("(c p) j -> p c j", p=128)
        t_wq = B.dma("gpsimd", "d_wq", WLD[0].rearrange("p (c j) -> p c j", j=1024), wld_ap(wq_h))
        t_wk = B.dma("gpsimd", "d_wk", WLD[1].rearrange("p (c j) -> p c j", j=1024), wld_ap(wk_h))

        stats_tok = {}
        rstd_tok = {}
        mrs_tok = {}
        xn_tok = {}
        store_tok = {}

        s1_free = {}
        s2_free = {}

        def emit_stats(i):
            p = i % 4
            q = i % 2
            ta = B.emit("scalar", lambda s, p=p, q=q: s.activation(
                SQ[:, :], XT[p][:, :], AF.Square, accum_out=S2c[q]),
                waits=[t_ld_tok[i], s2_free.get(i - 2)], inc=True)
            tb = B.emit("scalar", lambda s, p=p, q=q: s.activation(
                SQ2[:, :], XT[p][:, :], AF.Identity, accum_out=S1c[q]),
                waits=[s1_free.get(i - 2)], inc=True)
            tv = B.emit("vector", lambda v, q=q: v.tensor_scalar(MVc[q], S1c[q], INV1024, None, ALU.mult),
                        waits=[tb, mrs_tok.get(i - 2)], inc=True)
            s1_free[i] = tv
            tv = B.emit("vector", lambda v, q=q: v.tensor_tensor(MSc[q], MVc[q], MVc[q], ALU.mult),
                        waits=[tv], inc=True)
            tv = B.emit("vector", lambda v, q=q: v.scalar_tensor_tensor(
                VVc[q], S2c[q], INV1024, MSc[q], ALU.mult, ALU.subtract),
                waits=[tv, rstd_tok.get(i - 2)], inc=True)
            s2_free[i] = tv
            tv = B.emit("vector", lambda v, q=q: v.tensor_scalar(VVc[q], VVc[q], 1e-5, None, ALU.add),
                        waits=[tv], inc=True)
            stats_tok[i] = tv

        def emit_rstd(i):
            q = i % 2
            ta = B.emit("scalar", lambda s, q=q: s.activation(VVc[q], VVc[q], AF.Ln),
                        waits=[stats_tok[i]], inc=True)
            rstd_tok[i] = B.emit("scalar", lambda s, i=i, q=q: s.activation(
                STAT[:, i:i + 1], VVc[q], AF.Exp, scale=-0.5), waits=[ta], inc=True)

        def emit_mrs(i):
            q = i % 2
            mrs_tok[i] = B.emit("vector", lambda v, i=i, q=q: v.scalar_tensor_tensor(
                STAT[:, 32 + i:33 + i], MVc[q], -1.0, STAT[:, i:i + 1], ALU.mult, ALU.mult),
                waits=[rstd_tok[i]], inc=True)

        def emit_xn(i):
            p = i % 4
            rb = STAT[:, i:i + 1].rearrange("p (b o) -> p b o", o=1).broadcast_to([128, 1, 1024])
            mb = STAT[:, 32 + i:33 + i].rearrange("p (b o) -> p b o", o=1).broadcast_to([128, 1, 1024])
            t1 = B.emit("vector", lambda v, p=p, rb=rb: v.tensor_tensor(
                XT[p][:, :].rearrange("p (b w) -> p b w", w=1024),
                XT[p][:, :].rearrange("p (b w) -> p b w", w=1024), rb, ALU.mult),
                waits=[mrs_tok[i]], inc=True)
            t2 = B.emit("vector", lambda v, i=i, p=p, mb=mb: v.tensor_tensor(
                XNBF[i % 2][:, :].rearrange("p (b w) -> p b w", w=1024),
                XT[p][:, :].rearrange("p (b w) -> p b w", w=1024), mb, ALU.add),
                waits=[t1, store_tok.get(i - 2)], inc=True)
            xn_tok[i] = t2
            store_tok[i] = B.dma("gpsimd", "d_xnw%d" % (i % 2), xn_d[i * 128:(i + 1) * 128, :],
                                 XNBF[i % 2][:, :], waits=[t2])

        t_ld_tok = {}
        t_xnt_cb = {}
        pb8 = [PSA[:, k * 512:(k + 1) * 512] for k in range(4)] + \
              [PSB[:, k * 512:(k + 1) * 512] for k in range(4)]
        ev_tok = {}
        mm_tok = {}
        kst_dma = {}
        last_ev_bank = {}
        P2S = {"g": 0, "kn": 0, "q_last": None}
        pend_evac = []

        def emit_q_group(oc, cb):
            g = P2S["g"]; P2S["g"] += 1
            w0 = [t_xnt_cb[cb], t_wq, ev_tok.get(g - 8)]
            tt = None
            for c in range(8):
                tt = B.emit("tensor", lambda t, c=c, g=g, oc=oc, cb=cb:
                            t.matmul(pb8[g % 8],
                                     WLD[0][:, c * 1024 + oc * 128: c * 1024 + (oc + 1) * 128],
                                     XNT[:, c * NT + cb * 512: c * NT + cb * 512 + 512],
                                     start=(c == 0), stop=(c == 7)),
                            waits=(w0 if c == 0 else []), inc=(c == 7))
            mm_tok[g] = tt
            P2S["q_last"] = tt

            def do_evac(g=g, oc=oc, cb=cb, tt=tt):
                ev_tok[g] = B.emit("scalar", lambda s:
                                   s.activation(QT[:, oc * NO + cb * 512: oc * NO + cb * 512 + 512],
                                                pb8[g % 8], AF.Identity),
                                   waits=[tt], inc=True)
                last_ev_bank[g % 8] = ev_tok[g]
            pend_evac.append(do_evac)

        def emit_k_group(oc, cb):
            g = P2S["g"]; P2S["g"] += 1
            kn = P2S["kn"]; P2S["kn"] += 1
            w0 = [t_xnt_cb[cb], t_wk, ev_tok.get(g - 8)]
            tt = None
            for c in range(8):
                tt = B.emit("tensor", lambda t, c=c, g=g, oc=oc, cb=cb:
                            t.matmul(pb8[g % 8],
                                     WLD[1][:, c * 1024 + oc * 128: c * 1024 + (oc + 1) * 128],
                                     XNT[:, c * NT + cb * 512: c * NT + cb * 512 + 512],
                                     start=(c == 0), stop=(c == 7)),
                            waits=(w0 if c == 0 else []), inc=(c == 7))
            mm_tok[g] = tt

            def do_evac(g=g, oc=oc, cb=cb, kn=kn, tt=tt):
                j = kn % 4
                ev_tok[g] = B.emit("scalar", lambda s:
                                   s.activation(KST[j][:, :], pb8[g % 8], AF.Identity),
                                   waits=[tt, kst_dma.get(kn - 4)], inc=True)
                last_ev_bank[g % 8] = ev_tok[g]
                kst_dma[kn] = B.dma("gpsimd", "d_ks%d" % (kn % 4),
                                    kt_d[oc * 128:(oc + 1) * 128, cb * 512:(cb + 1) * 512],
                                    KST[j][:, :], waits=[ev_tok[g]])
            pend_evac.append(do_evac)
        for i in range(NTILE + 1):
            if i < NTILE:
                t_ld_tok[i] = B.dma("sync", "d_x%d" % (i % 4), XT[i % 4][:, :],
                                    x_h[i * 128:(i + 1) * 128, :], waits=[xn_tok.get(i - 4)])
                emit_stats(i)
                emit_rstd(i)
            if i >= 1:
                emit_mrs(i - 1)
                emit_xn(i - 1)
            if i >= 4 and i % 4 == 0:
                cb = i // 4 - 1
                for fn in pend_evac:
                    fn()
                pend_evac.clear()
                tok = None
                for c in range(8):
                    tok = B.dma("sync", "d_xnt%d" % (cb % 2),
                                XNT[:, c * NT + cb * 512: c * NT + cb * 512 + 512],
                                xn_d[cb * 512:(cb + 1) * 512, c * 128:(c + 1) * 128],
                                waits=[store_tok[4 * cb + 3], store_tok[4 * cb + 2]],
                                transpose=True)
                t_xnt_cb[cb] = tok
                if cb < 4:
                    for oc in range(8):
                        emit_q_group(oc, cb)
            if i >= 6 and i % 4 == 2 and (i - 6) // 4 <= 6:
                for fn in pend_evac:
                    fn()
                pend_evac.clear()
                for oc in range(8):
                    emit_k_group(oc, (i - 6) // 4)
        for fn in pend_evac:
            fn()
        pend_evac.clear()
        for oc in range(8):
            emit_k_group(oc, 7)
        for fn in pend_evac:
            fn()
        pend_evac.clear()
        t_xnt = t_xnt_cb[7]

        t_wv = B.dma("gpsimd", "d_wv", WLD[0].rearrange("p (c j) -> p c j", j=1024), wld_ap(wv_h),
                     waits=[P2S["q_last"]])
        vst_dma = {}
        v15_mm = None
        for ti in range(32):
            for heb in range(1):
                g = P2S["g"]; P2S["g"] += 1
                vg = ti
                w0 = [t_xnt_cb[ti // 4], t_wv, ev_tok.get(g - 8)]
                tt = None
                for c in range(8):
                    tt = B.emit("tensor", lambda t, c=c, g=g, ti=ti, heb=heb:
                                t.matmul(pb8[g % 8],
                                         XNT[:, c * NT + ti * 128: c * NT + (ti + 1) * 128],
                                         WLD[0][:, c * 1024 + heb * 512: c * 1024 + (heb + 1) * 512],
                                         start=(c == 0), stop=(c == 7)),
                                waits=(w0 if c == 0 else []), inc=(c == 7))
                mm_tok[g] = tt
                if ti == 15:
                    v15_mm = tt
                j = vg % 4
                ev_tok[g] = B.emit("scalar", lambda s, g=g, j=j: s.activation(VST[j][:, :], pb8[g % 8], AF.Identity),
                                   waits=[tt, vst_dma.get(vg - 4)], inc=True)
                last_ev_bank[g % 8] = ev_tok[g]
                vst_dma[vg] = B.dma("gpsimd", "d_vs%d" % (vg % 4),
                                    v_d[ti, heb * 8:(heb + 1) * 8].rearrange("h p e -> p h e"),
                                    VST[j][:, :].rearrange("p (h e) -> p h e", e=64),
                                    waits=[ev_tok[g]])
        pe_p2_done = mm_tok[P2S["g"] - 1]
        WVX = [ABF[:, 25600 + c * 4096: 25600 + c * 4096 + 512] for c in range(8)]
        t_wvx = None
        for c in range(8):
            t_wvx = B.dma("gpsimd", "d_wv", WVX[c], wv_h[c * 128:(c + 1) * 128, 512:1024],
                          waits=([v15_mm] if c == 0 else []))
        pe_p2_cnt = B.cnt["tensor"]
        p2_stores = [(B.dma_sem(s), B.dcnt[s]) for s in
                     ("d_ks0", "d_ks1", "d_ks2", "d_ks3", "d_vs0", "d_vs1", "d_vs2", "d_vs3")]

        tva = None
        for VA in (VA1[0], VA1[1], VA2[0], VA2[1]):
            tva = B.emit("vector", lambda v, VA=VA: v.memset(
                VA.rearrange("p (c o) -> p c o", o=65)[:, :, 64:65], 1.0), waits=[pe_p2_done], inc=True)
        tva = B.emit("vector", lambda v: v.memset(ONE_A, 1.0), waits=[tva], inc=True)
        tva = B.emit("vector", lambda v: v.memset(ONE_B, 1.0), waits=[tva], inc=True)
        tva = B.emit("vector", lambda v: v.memset(ONE_C, 1.0), waits=[tva], inc=True)
        t_bias = B.dma("sync", "d_misc", BIASR, bias_h.rearrange("(o d) -> o d", o=1), waits=[tva])

        kp_tok = {}
        va_tok = {}
        for hp0 in (0, 1):
            w = p2_stores + [tva]
            kp_tok[hp0] = B.dma("sync", "d_kp%d" % hp0, KP[hp0][:, :], kt_d[hp0 * 128:(hp0 + 1) * 128, :], waits=w)
            va_tok[hp0] = []
            for (vi, (VA, h)) in enumerate(((VA1[hp0], 2 * hp0), (VA2[hp0], 2 * hp0 + 1))):
                va_tok[hp0].append(B.dma("sync", "d_va%d_%d" % (vi, hp0),
                                         VA.rearrange("p (c o) -> p c o", o=65)[:, :, 0:64],
                                         v_d[:, h].rearrange("c p e -> p c e"), waits=w))

        SPp = [PSA[:, 0:1024], PSA[:, 1024:2048]]
        ASET = [(PSB[0:65, 0:512], PSB[0:65, 512:1024])]
        pb2 = [PSB[:, 1024:1536], PSB[:, 1536:2048]]
        VSF = [AFP[:, 11264:11776], AFP[:, 11776:12288]]
        vd_state = {"n": 0, "ev": {}, "st": {}, "last_mm": None}
        vd_queue = list(range(32))

        def vd_group(ti):
            g2 = vd_state["n"]
            vd_state["n"] += 1
            w0 = [vd_state["ev"].get(g2 - 2), t_wvx, t_xnt]
            if g2 < 2:
                w0 += [last_ev_bank.get(6), last_ev_bank.get(7)]
            tt = None
            for c in range(8):
                tt = B.emit("tensor", lambda t, c=c, ti=ti, g2=g2: t.matmul(
                    pb2[g2 % 2], XNT[:, c * NT + ti * 128: c * NT + (ti + 1) * 128],
                    WVX[c], start=(c == 0), stop=(c == 7)),
                    waits=(w0 if c == 0 else []), inc=(c == 7))
            vd_state["last_mm"] = tt
            ev = B.emit("vector", lambda v, g2=g2: v.tensor_copy(VSF[g2 % 2], pb2[g2 % 2]),
                        waits=[tt, vd_state["st"].get(g2 - 2)], inc=True)
            vd_state["ev"][g2] = ev
            vd_state["st"][g2] = B.dma("gpsimd", "d_vs%d" % (g2 % 2),
                                       v_d[ti, 8:16].rearrange("h p e -> p h e"),
                                       VSF[g2 % 2].rearrange("p (h e) -> p h e", e=64),
                                       waits=[ev])
        ZBV = [(PSB[0:64, 0:512], PSB[0:64, 512:1024]),
               (PSB[0:64, 1024:1536], PSB[0:64, 1536:2048])]
        pe_hp_last = {}
        prev_exp_for_buf = [None, None]
        prev_av_for_E = [None, None, None, None]
        prev_set_tail = [None, None]
        z_prev = {1: None, 2: None}
        src_free = {}
        chainsrc = {}
        xnq_tok = {}
        chain_state = {"last": None, "prev_store": None}
        xd_store = []
        wd_store = []
        az_prev = {}
        qload_tok = {}

        def quant_chain_steps(src_ap, dst_dram, start_wait):
            toks = {}
            steps = []
            st = {"tv": None}
            w_first = [start_wait, chain_state["last"]]
            ps = chain_state["prev_store"]
            iscb = QIS.rearrange("p (b o) -> p b o", o=1).broadcast_to([128, 32, 32])
            scb = QSC.rearrange("p (b o) -> p b o", o=1).broadcast_to([128, 32, 32])

            def op(build, key=None):
                def run(build=build, key=key):
                    w = w_first if st["tv"] is None else [st["tv"]]
                    st["tv"] = B.emit("vector", build, waits=w, inc=True)
                    if key:
                        toks[key] = st["tv"]
                steps.append(run)

            op(lambda v, src_ap=src_ap: v.tensor_reduce(
                QBM, src_ap.rearrange("p (b w) -> p b w", w=32), AX.X, ALU.max,
                apply_absolute_value=True))
            op(lambda v: v.tensor_scalar(QSC, QBM, INV75, 1e-30, ALU.mult, ALU.max))
            op(lambda v: v.reciprocal(QIS, QSC))
            op(lambda v, src_ap=src_ap, iscb=iscb: v.tensor_tensor(
                U_.rearrange("p (b w) -> p b w", w=32),
                src_ap.rearrange("p (b w) -> p b w", w=32), iscb, ALU.mult), key="src_free")
            op(lambda v: v.tensor_scalar(U_, U_, 7.5, -7.5, ALU.min, ALU.max))
            op(lambda v: v.tensor_scalar(PQ1, U_, -1.0, None, ALU.mult))
            op(lambda v: v.tensor_tensor(AU_, U_, PQ1, ALU.max))
            op(lambda v: v.tensor_scalar(PQ1, AU_, 2.0, -4.0, ALU.is_ge, ALU.mult))
            op(lambda v: v.tensor_scalar(PQ2, AU_, 4.0, -2.0, ALU.is_ge, ALU.mult))
            op(lambda v: v.scalar_tensor_tensor(ISTq, PQ1, 8.0, PQ2, ALU.add, ALU.add))
            op(lambda v: v.tensor_scalar(PQ1, AU_, 2.0, 0.125, ALU.is_ge, ALU.mult))
            op(lambda v: v.tensor_scalar(PQ2, AU_, 4.0, 0.25, ALU.is_ge, ALU.mult))
            op(lambda v: v.scalar_tensor_tensor(STPq, PQ1, 0.125, PQ2, ALU.add, ALU.add))
            op(lambda v: v.tensor_tensor(TQ, AU_, ISTq, ALU.mult))
            op(lambda v: v.tensor_scalar(X5, TQ, 0.5, None, ALU.add))
            for wgt in (16.0, 8.0, 4.0, 2.0, 1.0):
                op(lambda v, wgt=wgt: v.tensor_scalar(PQ1, X5, wgt, -wgt, ALU.is_ge, ALU.mult))
                op(lambda v: v.tensor_tensor(X5, X5, PQ1, ALU.add))
            op(lambda v: v.tensor_scalar(RQ, TQ, 0.5, None, ALU.add))
            op(lambda v: v.tensor_tensor(RQ, RQ, X5, ALU.subtract))
            op(lambda v: v.tensor_tensor(RQ, RQ, STPq, ALU.mult))
            op(lambda v: v.tensor_scalar(PQ1, U_, 0.0, 2.0, ALU.is_ge, ALU.mult))
            op(lambda v: v.tensor_tensor(PQ2, RQ, PQ1, ALU.mult))
            op(lambda v: v.tensor_tensor(RQ, PQ2, RQ, ALU.subtract))

            def run_final():
                tv = B.emit("vector", lambda v, scb=scb: v.tensor_tensor(
                    XDBF.rearrange("p (b w) -> p b w", w=32),
                    RQ.rearrange("p (b w) -> p b w", w=32), scb, ALU.mult),
                    waits=[st["tv"], ps], inc=True)
                toks["last"] = tv
                chain_state["last"] = tv
                toks["store"] = B.dma("gpsimd", "d_xd", dst_dram, XDBF, waits=[tv])
                chain_state["prev_store"] = toks["store"]
            steps.append(run_final)
            return toks, steps

        def task_load(u):
            if u < 16:
                qload_tok[u] = B.dma("sync", "d_xq%d" % (u % 2), XQ[u % 2][:, :],
                                     x_h[u * 128:(u + 1) * 128, :], waits=[src_free.get(u - 2)])
            elif u < 24:
                j = u - 16
                qload_tok[u] = B.dma("sync", "d_xq%d" % (u % 2), XQ[u % 2][:, :],
                                     wf_h[j * 128:(j + 1) * 128, :], waits=[src_free.get(u - 2)])

        task_load(0)
        for u in range(32):
            hp, qgi = divmod(u, 4)
            h1, h2 = 2 * hp, 2 * hp + 1
            bufK = hp % 2
            aset = 0
            A1, A2 = ASET[aset]
            if qgi == 0 and 1 <= hp <= 6:
                nb = (hp + 1) % 2
                w = [pe_hp_last[hp - 1]]
                kp_tok[hp + 1] = B.dma("sync", "d_kp%d" % nb, KP[nb][:, :],
                                       kt_d[(hp + 1) * 128:(hp + 2) * 128, :], waits=w)
                va_tok[hp + 1] = []
                wv2 = list(w)
                if hp + 1 >= 4:
                    wv2 += [vd_state["st"][31], vd_state["st"][30]]
                for (vi, (VA, h)) in enumerate(((VA1[nb], 2 * (hp + 1)), (VA2[nb], 2 * (hp + 1) + 1))):
                    va_tok[hp + 1].append(B.dma("sync", "d_va%d_%d" % (vi, nb),
                                                VA.rearrange("p (c o) -> p c o", o=65)[:, :, 0:64],
                                                v_d[:, h].rearrange("c p e -> p c e"), waits=wv2))
            unit_ready = [kp_tok[hp]] + va_tok[hp]
            task_load(u + 1)

            if u < 16:
                t_xnq = B.emit("scalar", lambda s, u=u: s.activation(
                    XNQR[u % 2][:, :], XQ[u % 2][:, :], AF.Identity,
                    scale=STAT[:, u:u + 1], bias=STAT[:, 32 + u:33 + u]),
                    waits=[qload_tok[u], chainsrc.get(u - 2)], inc=True)
                xnq_tok[u] = t_xnq
            chain_steps = None
            ctoks = None
            if u < 16:
                ctoks, chain_steps = quant_chain_steps(
                    XNQR[u % 2][:, :], xd_d[u * 128:(u + 1) * 128, :], xnq_tok[u])
                src_free[u] = xnq_tok[u]
            elif u < 24:
                j2 = u - 16
                ctoks, chain_steps = quant_chain_steps(
                    XQ[u % 2][:, :], wd_d[j2 * 128:(j2 + 1) * 128, :], qload_tok[u])
            if u == 13:
                p3_wp = B.dma("gpsimd", "d_wp", WPB.rearrange("p (c j) -> p c j", j=1024),
                              wld_ap(wp_h), waits=[vd_state["last_mm"]])
            if u == 17:
                p3_xdt = None
                for ic in range(8):
                    p3_xdt = B.dma("sync", "d_xdt2", XDT[:, ic * NO:(ic + 1) * NO],
                                   xd_d[:, ic * 128:(ic + 1) * 128],
                                   waits=[xd_store[-1], vd_state["last_mm"] if ic == 0 else None],
                                   transpose=True)
            if u == 25:
                p3_wdt = None
                for ic in range(8):
                    p3_wdt = B.dma("sync", "d_wdt2", WDT[:, ic * D:(ic + 1) * D],
                                   wd_d[:, ic * 128:(ic + 1) * 128],
                                   waits=[wd_store[-1], vd_state["last_mm"] if ic == 0 else None],
                                   transpose=True)

            qcol = hp * NO + qgi * 512
            exp_tok = {}
            sc_tok = {}
            av_list = {}

            def do_av(t2, A1=A1, A2=A2, bufK=bufK, aset=aset):
                wav = [exp_tok[t2]]
                if t2 == 0:
                    wav.append(prev_set_tail[aset])
                    if u == 0:
                        wav += [last_ev_bank.get(4), last_ev_bank.get(5)]
                B.emit("tensor", lambda t, t2=t2, A1=A1, bufK=bufK: t.matmul(
                    A1, VA1[bufK][:, t2 * 65:t2 * 65 + 65],
                    EB4[t2 % 4][:, 0:512],
                    start=(t2 == 0), stop=(t2 == 31)), waits=wav, inc=False)
                av_list[t2] = B.emit("tensor", lambda t, t2=t2, A2=A2, bufK=bufK: t.matmul(
                    A2, VA2[bufK][:, t2 * 65:t2 * 65 + 65],
                    EB4[t2 % 4][:, 512:1024],
                    start=(t2 == 0), stop=(t2 == 31)), waits=[], inc=True)

            for tc in range(32):
                b = tc % 2
                if tc < 2:
                    w_sc = [prev_exp_for_buf[b]]
                    if tc == 0:
                        w_sc += unit_ready
                    if u == 0:
                        w_sc += [last_ev_bank.get(2 * b), last_ev_bank.get(2 * b + 1)]
                else:
                    w_sc = [exp_tok[tc - 2]]
                B.emit("tensor", lambda t, tc=tc, b=b, bufK=bufK, qcol=qcol: t.matmul(
                    SPp[b][:, 0:512],
                    KP[bufK][0:64, tc * 128:(tc + 1) * 128],
                    QT[0:64, qcol:qcol + 512],
                    start=True, stop=True, tile_position=(0, 0)), waits=w_sc, inc=False)
                sc_tok[tc] = B.emit("tensor", lambda t, tc=tc, b=b, bufK=bufK, qcol=qcol: t.matmul(
                    SPp[b][:, 512:1024],
                    KP[bufK][64:128, tc * 128:(tc + 1) * 128],
                    QT[64:128, qcol:qcol + 512],
                    start=True, stop=True, tile_position=(64, 0)), waits=[], inc=True)
                we = [sc_tok[tc]]
                if tc >= 4:
                    we.append(av_list[tc - 4])
                else:
                    we.append(prev_av_for_E[tc % 4])
                exp_tok[tc] = B.emit("scalar", lambda s, b=b, tc=tc: s.activation(
                    EB4[tc % 4][:, 0:1024], SPp[b][:, :], AF.Exp, scale=0.125), waits=we, inc=True)
                if tc >= 2:
                    do_av(tc - 2)
                if u <= 11 and tc in (6, 16, 26) and vd_queue:
                    vd_group(vd_queue.pop(0))
                if chain_steps and tc < 27:
                    chain_steps.pop(0)()
            do_av(30)
            do_av(31)
            prev_exp_for_buf = [exp_tok[30], exp_tok[31]]
            prev_av_for_E = [av_list[28], av_list[29], av_list[30], av_list[31]]
            if qgi == 3:
                pe_hp_last[hp] = av_list[31]

            tva_ = B.emit("vector", lambda v: v.tensor_copy(ASF2[:, :], PSB[0:64, 0:1024]),
                          waits=[av_list[31], az_prev.get(1), az_prev.get(2)], inc=True)
            tz = {}
            for hs, Abank, hh in ((1, A1, h1), (2, A2, h2)):
                tz[hs] = B.emit("vector", lambda v, hs=hs, Abank=Abank: v.tensor_copy(
                    ZR[hs], Abank[64:65, :]), waits=[tva_ if hs == 1 else tz[1], z_prev.get(hs)], inc=True)
                az_prev[hs] = B.dma("gpsimd", "d_az%d" % hs,
                                    a_d[qgi * 4:(qgi + 1) * 4, hh].rearrange("b p j -> p b j"),
                                    ASF2[:, (hs - 1) * 512:hs * 512].rearrange("p (b j) -> p b j", j=128),
                                    waits=[tva_])
                z_prev[hs] = B.dma("gpsimd", "d_z%d" % hs,
                                   z_d[hh:hh + 1, qgi * 512:(qgi + 1) * 512], ZR[hs], waits=[tz[hs]])
            prev_set_tail[aset] = tz[2]
            while chain_steps:
                chain_steps.pop(0)()

            if u < 16:
                chainsrc[u] = ctoks["src_free"]
                xd_store.append(ctoks["store"])
            elif u < 24:
                src_free[u] = ctoks["src_free"]
                wd_store.append(ctoks["store"])
        last_chain = chain_state["last"]

        t_wp = p3_wp
        ZV = afp(24, 128, 256)
        ZRV = afp(25, 128, 256)
        t_zv = B.dma("sync", "d_zv", ZV, z_d.rearrange("h (a q) -> (h a) q", a=8),
                     waits=[z_prev[1], z_prev[2], last_chain])
        t_zr = B.emit("vector", lambda v: v.reciprocal(ZRV, ZV), waits=[t_zv], inc=True)
        t_zrs = B.dma("gpsimd", "d_zr", zr_d.rearrange("h (a q) -> (h a) q", a=8), ZRV, waits=[t_zr])
        t_xdt = p3_xdt
        t_wdt = p3_wdt

        PPq = [[PSA[:, 0:512], PSA[:, 512:1024]], [PSA[:, 1024:1536], PSA[:, 1536:2048]]]
        PFq = PSB[:, 0:1024]
        DP = PSB[:, 1024:2048]
        ZPK0 = [AFP[0:1, 13312:14336], AFP[0:1, 14464:15488]]
        ZPK1 = [AFP[64:65, 12288:13312], AFP[64:65, 14464:15488]]
        dmm_tok = {}
        dmul_tok = {}
        xn4_tok = {}
        pmm_tok = {}
        sadd_tok = {}
        gelu_tok = {}
        ffn_tok = {}
        silu_tok = {}
        sg_tok = {}
        outtok = {}
        out_store = {}
        xb_tok = {}

        def p4_stage1(qt):
            pr = qt % 2
            w_z = [dmm_tok.get(qt - 2),
                   t_zrs if qt <= 1 else None,
                   az_prev[1] if qt <= 1 else None,
                   az_prev[2] if qt <= 1 else None]
            tz = None
            for pg in range(2):
                dst = (ZPK0 if pg == 0 else ZPK1)[pr].rearrange("o (c j) -> o c j", j=128)
                srcz = zr_d[pg::2, qt * 128:(qt + 1) * 128].rearrange("(o c) j -> o c j", o=1)
                tz = B.dma("sync", "d_dt%d" % pr, dst, srcz, waits=(w_z if pg == 0 else []))
            t_x4 = B.dma("sync", "d_p4x%d" % pr, XA[pr][:, :], x_h[qt * 128:(qt + 1) * 128, :],
                         waits=[xn4_tok.get(qt - 2),
                                last_chain if qt <= 1 else None,
                                xnq_tok[15] if qt <= 1 else None])
            w_at = [dmul_tok.get(qt - 2), t_wp,
                    t_zrs if qt <= 1 else None,
                    wd_store[-1] if qt == 0 else None,
                    az_prev[1] if qt == 0 else None,
                    az_prev[2] if qt == 0 else None,
                    pe_hp_last[7] if qt == 0 else None]
            t_at_e = B.dma("sync", "d_p4a%d" % pr,
                           ATB[pr][0:64, :].rearrange("p (c j) -> p c j", j=128),
                           a_d[qt].rearrange("(c t) p j -> t c p j", t=2)[0].rearrange("c p j -> p c j"),
                           waits=w_at)
            t_at_o = B.dma("sync", "d_p4a%d" % pr,
                           ATB[pr][64:128, :].rearrange("p (c j) -> p c j", j=128),
                           a_d[qt].rearrange("(c t) p j -> t c p j", t=2)[1].rearrange("c p j -> p c j"),
                           waits=[])
            if qt >= 1:
                pr2 = (qt - 1) % 2
                xb_tok[qt - 1] = B.dma("sync", "d_p4b%d" % pr2, XB[pr2][:, :],
                                       x_h[(qt - 1) * 128:qt * 128, :],
                                       waits=[outtok.get(qt - 3),
                                              last_chain if qt - 1 <= 1 else None])
            w_dm = [tz, dmul_tok.get(qt - 1),
                    vd_state["ev"][31] if qt == 0 else None]
            tmm = None
            for pg in range(2):
                ones = ONE_A if pg == 0 else ONE_C
                park = (ZPK0 if pg == 0 else ZPK1)[pr]
                for nh in range(2):
                    tmm = B.emit("tensor", lambda t, pg=pg, nh=nh, ones=ones, park=park:
                                 t.matmul(DP[pg * 64:(pg + 1) * 64, nh * 512:(nh + 1) * 512],
                                          ones[:, 0:64], park[:, nh * 512:(nh + 1) * 512],
                                          start=True, stop=True),
                                 waits=(w_dm if (pg == 0 and nh == 0) else []),
                                 inc=(pg == 1 and nh == 1))
            dmm_tok[qt] = tmm
            t_xn4 = B.emit("scalar", lambda s, qt=qt, pr=pr: s.activation(
                XNQ4[qt % 2][:, :], XA[pr][:, :], AF.Identity,
                scale=STAT[:, qt:qt + 1], bias=STAT[:, 32 + qt:33 + qt]),
                waits=[t_x4, sadd_tok.get(qt - 2),
                       az_prev[1] if qt <= 1 else None,
                       az_prev[2] if qt <= 1 else None], inc=True)
            xn4_tok[qt] = t_xn4
            t_dm = B.emit("vector", lambda v, pr=pr: v.tensor_tensor(
                DTp[pr][:, :], ATB[pr][:, :], DP[:, :], ALU.mult),
                waits=[t_at_o, dmm_tok[qt], gelu_tok.get(qt - 2)], inc=True)
            dmul_tok[qt] = t_dm

        def p4_stage2(qt):
            pr = qt % 2
            t_gelu = B.emit("scalar", lambda s, qt=qt, pr=pr: s.activation(
                ATT[pr][:, :], DTp[pr][:, :], AF.Gelu),
                waits=[dmul_tok[qt], pmm_tok.get(qt - 2)], inc=True)
            gelu_tok[qt] = t_gelu
            ts_list = []
            for half in range(2):
                w0 = [t_gelu, sadd_tok.get(qt - 2)]
                tt = None
                for hc in range(8):
                    tt = B.emit("tensor", lambda t, hc=hc, half=half, qt=qt, pr=pr:
                                t.matmul(PPq[pr][half], ATT[pr][:, hc * 128:(hc + 1) * 128],
                                         WPB[:, hc * 1024 + half * 512: hc * 1024 + half * 512 + 512],
                                         start=(hc == 0), stop=(hc == 7)),
                                waits=(w0 if hc == 0 else []), inc=(hc == 7))
                if half == 1:
                    pmm_tok[qt] = tt
                ts_ = B.emit("vector", lambda v, half=half, pr=pr, qt=qt: v.tensor_tensor(
                    STLp[pr][:, half * 512:(half + 1) * 512], PPq[pr][half],
                    XNQ4[qt % 2][:, half * 512:(half + 1) * 512],
                    ALU.add), waits=[tt, xn4_tok[qt], sg_tok.get(qt - 2)], inc=True)
                ts_list.append(ts_)
            sadd_tok[qt] = ts_list[-1]

        def emit_ffn(qt):
            w0 = [t_xdt, t_wdt, silu_tok.get(qt - 1)]
            if qt == 0:
                w0.append(prev_set_tail[0])
                w0.append(t_bias)
            tt = None
            for half in range(2):
                for ic in range(8):
                    tt = B.emit("tensor", lambda t, ic=ic, half=half, qt=qt:
                                t.matmul(PFq[:, half * 512:(half + 1) * 512],
                                         XDT[:, ic * NO + qt * 128: ic * NO + (qt + 1) * 128],
                                         WDT[:, ic * 1024 + half * 512: ic * 1024 + half * 512 + 512],
                                         start=(ic == 0), stop=False),
                                waits=(w0 if (half == 0 and ic == 0) else []), inc=False)
                tt = B.emit("tensor", lambda t, half=half:
                            t.matmul(PFq[:, half * 512:(half + 1) * 512],
                                     ONE_C[:, :], BIASR[:, half * 512:half * 512 + 512],
                                     start=False, stop=True),
                            waits=[], inc=(half == 1))
            ffn_tok[qt] = tt

        def p4_stage3(qt):
            pr = qt % 2
            t_silu = B.emit("scalar", lambda s, pr=pr: s.activation(GSp[pr][:, :], PFq[:, :], AF.Silu),
                            waits=[ffn_tok[qt], out_store.get(qt - 2)], inc=True)
            silu_tok[qt] = t_silu
            if qt + 1 < OTILE:
                emit_ffn(qt + 1)
            tv = B.emit("vector", lambda v, pr=pr: v.tensor_tensor(
                GSp[pr][:, :], GSp[pr][:, :], STLp[pr][:, :], ALU.mult),
                waits=[t_silu, sadd_tok[qt]], inc=True)
            sg_tok[qt] = tv
            tv = B.emit("vector", lambda v, qt=qt, pr=pr: v.tensor_tensor(
                GSp[pr][:, :], GSp[pr][:, :], XB[pr][:, :], ALU.add),
                waits=[tv, xb_tok[qt]], inc=True)
            outtok[qt] = tv
            out_store[qt] = B.dma("gpsimd", "d_out%d" % pr, out_h[qt * 128:(qt + 1) * 128, :],
                                  GSp[pr][:, :], waits=[tv])

        p4_stage1(0)
        p4_stage1(1)
        emit_ffn(0)
        for qt in range(OTILE):
            p4_stage2(qt)
            if qt >= 1:
                p4_stage3(qt - 1)
            if qt + 2 < OTILE:
                p4_stage1(qt + 2)
        qlast = OTILE - 1
        xb_tok[qlast] = B.dma("sync", "d_p4b%d" % (qlast % 2), XB[qlast % 2][:, :],
                              x_h[qlast * 128:(qlast + 1) * 128, :],
                              waits=[outtok.get(qlast - 2)])
        p4_stage3(qlast)
        B.wait("gpsimd", [out_store[OTILE - 1], out_store[OTILE - 2]])
        B.wait("sync", [out_store[OTILE - 1], out_store[OTILE - 2]])

        @blk.sync
        def _(sync):
            for fn in B.ops["sync"]:
                fn(sync)

        @blk.gpsimd
        def _(g):
            for fn in B.ops["gpsimd"]:
                fn(g)

        @blk.vector
        def _(v):
            for fn in B.ops["vector"]:
                fn(v)

        @blk.scalar
        def _(s):
            for fn in B.ops["scalar"]:
                fn(s)

        @blk.tensor
        def _(t):
            for fn in B.ops["tensor"]:
                fn(t)

    return nc


_NC_CACHE = []


def kernel(L0_attn_X, L0_attn_Wq, L0_attn_Wk, L0_attn_Wv, L0_attn_Wpost,
           L0_ffn_weight, L0_ffn_bias):
    x = np.ascontiguousarray(np.asarray(L0_attn_X, dtype=np.float32))
    wq = np.ascontiguousarray(np.asarray(L0_attn_Wq, dtype=np.float32))
    wk = np.ascontiguousarray(np.asarray(L0_attn_Wk, dtype=np.float32))
    wv = np.ascontiguousarray(np.asarray(L0_attn_Wv, dtype=np.float32))
    wp = np.ascontiguousarray(np.asarray(L0_attn_Wpost, dtype=np.float32))
    wf = np.ascontiguousarray(np.asarray(L0_ffn_weight, dtype=np.float32))
    bias = np.ascontiguousarray(np.asarray(L0_ffn_bias, dtype=np.float32))

    if not _NC_CACHE:
        _NC_CACHE.append(build_program())
    nc = _NC_CACHE[0]

    in_maps = []
    for c in range(8):
        b, h = divmod(c, 2)
        own = x[b, h * 2048:(h + 1) * 2048]
        oth = x[b, (1 - h) * 2048:(2 - h) * 2048]
        xc = np.ascontiguousarray(np.concatenate([own, oth], axis=0))
        in_maps.append({"x": xc, "wq": wq, "wk": wk, "wv": wv,
                        "wpost": wp, "w": wf, "bias": bias})
    res = run_bass_kernel_spmd(nc, in_maps, core_ids=list(range(8)))
    out = np.empty((4, 4096, 1024), dtype=np.float32)
    for c in range(8):
        b, h = divmod(c, 2)
        out[b, h * 2048:(h + 1) * 2048] = res.results[c]["out"]
    return out


# version 25
# speedup vs baseline: 1.0821x; 1.0083x over previous
import numpy as np
import concourse.bass as bass
import concourse.mybir as mybir
from concourse.bass_utils import run_bass_kernel_spmd

F32 = mybir.dt.float32
BF16 = mybir.dt.bfloat16
AF = mybir.ActivationFunctionType
ALU = mybir.AluOpType
AX = mybir.AxisListType

NT = 4096
NO = 2048
D = 1024
NTILE = 32
OTILE = 16
INV75 = float(np.float32(1.0 / 7.5))
INV1024 = float(np.float32(1.0 / 1024.0))


class Builder:
    ENGINES = ("sync", "gpsimd", "vector", "scalar", "tensor")

    def __init__(self, nc, es):
        self.nc = nc
        self.es = es
        self.ops = {e: [] for e in self.ENGINES}
        self.sem = {}
        self.cnt = {}
        for e in ("vector", "scalar", "tensor"):
            self.sem[e] = es.enter_context(nc.semaphore("sem_" + e))
            self.cnt[e] = 0
        self.dsem = {}
        self.dcnt = {}

    def dma_sem(self, name):
        if name not in self.dsem:
            self.dsem[name] = self.es.enter_context(self.nc.semaphore(name))
            self.dcnt[name] = 0
        return self.dsem[name]

    def emit(self, eng, fn, waits=(), inc=False):
        w = [t for t in waits if t is not None]
        tok = None
        sem = None
        if inc:
            self.cnt[eng] += 1
            sem = self.sem[eng]
            tok = (sem, self.cnt[eng])

        def run(engine, w=w, fn=fn, sem=sem):
            for (s, v) in w:
                engine.wait_ge(s, v)
            instr = fn(engine)
            if sem is not None:
                instr.then_inc(sem, 1)
        self.ops[eng].append(run)
        return tok

    def dma(self, queue, stream, out, in_, waits=(), **kw):
        sem = self.dma_sem(stream)
        self.dcnt[stream] += 16
        tok = (sem, self.dcnt[stream])
        w = [t for t in waits if t is not None]

        def run(engine, w=w, out=out, in_=in_, sem=sem, kw=kw):
            for (s, v) in w:
                engine.wait_ge(s, v)
            engine.dma_start(out=out, in_=in_, **kw).then_inc(sem, 16)
        self.ops[queue].append(run)
        return tok

    def wait(self, eng, waits):
        w = [t for t in waits if t is not None]

        def run(engine, w=w):
            for (s, v) in w:
                engine.wait_ge(s, v)
        self.ops[eng].append(run)


def build_program():
    nc = bass.Bass("TRN2", target_bir_lowering=False)
    x_h = nc.dram_tensor("x", [NT, D], F32, kind="ExternalInput").ap()
    wq_h = nc.dram_tensor("wq", [D, D], F32, kind="ExternalInput").ap()
    wk_h = nc.dram_tensor("wk", [D, D], F32, kind="ExternalInput").ap()
    wv_h = nc.dram_tensor("wv", [D, D], F32, kind="ExternalInput").ap()
    wp_h = nc.dram_tensor("wpost", [D, D], F32, kind="ExternalInput").ap()
    wf_h = nc.dram_tensor("w", [D, D], F32, kind="ExternalInput").ap()
    bias_h = nc.dram_tensor("bias", [D], F32, kind="ExternalInput").ap()
    out_h = nc.dram_tensor("out", [NO, D], F32, kind="ExternalOutput").ap()
    xn_d = nc.dram_tensor("xn_d", [NT, D], BF16, kind="Internal").ap()
    kt_d = nc.dram_tensor("kt_d", [D, NT], BF16, kind="Internal").ap()
    v_d = nc.dram_tensor("v_d", [NTILE, 16, 128, 64], BF16, kind="Internal").ap()
    a_d = nc.dram_tensor("a_d", [OTILE, 16, 64, 128], BF16, kind="Internal").ap()
    xd_d = nc.dram_tensor("xd_d", [NO, D], BF16, kind="Internal").ap()
    wd_d = nc.dram_tensor("wd_d", [D, D], BF16, kind="Internal").ap()
    s_dd = nc.dram_tensor("s_dd", [NO, D], F32, kind="Internal").ap()
    z_d = nc.dram_tensor("z_d", [16, NO], F32, kind="Internal").ap()
    zr_d = nc.dram_tensor("zr_d", [16, NO], F32, kind="Internal").ap()

    from contextlib import ExitStack
    es = ExitStack()
    B = Builder(nc, es)

    ABF = es.enter_context(nc.sbuf_tensor("ABF", [128, 74752], BF16))
    AFP = es.enter_context(nc.sbuf_tensor("AFP", [128, 15488], F32))
    STAT = es.enter_context(nc.sbuf_tensor("STAT", [128, 176], F32))
    PSA = es.enter_context(nc.psum_tensor("PSA", [128, 2048], F32))
    PSB = es.enter_context(nc.psum_tensor("PSB", [128, 2048], F32))

    def abf(kb, p, n):
        return ABF[0:p, kb * 512: kb * 512 + n]

    def afp(kb, p, n):
        return AFP[0:p, kb * 256: kb * 256 + n]

    XNT = abf(50, 128, 8 * NT)
    QT = abf(114, 128, 8 * NO)
    WLD = [abf(0, 128, 8192), abf(16, 128, 8192)]
    XNBF = [abf(32, 128, 1024), abf(34, 128, 1024)]
    KST = [abf(36, 128, 512), abf(37, 128, 512), abf(42, 128, 512), abf(43, 128, 512)]
    VST = [abf(38, 128, 512), abf(39, 128, 512), abf(40, 128, 512), abf(41, 128, 512)]
    KP = [abf(0, 128, 4096), abf(8, 128, 4096)]
    VA1 = [abf(16, 128, 2080), abf(21, 128, 2080)]
    VA2 = [abf(26, 128, 2080), abf(31, 128, 2080)]
    EB = [abf(36, 128, 2048), abf(40, 128, 2048)]
    EB4 = [abf(36, 128, 1024), abf(38, 128, 1024), abf(40, 128, 1024), abf(42, 128, 1024)]
    XDBF = abf(48, 128, 1024)
    WPB = abf(50, 128, 8 * D)
    XDT = abf(66, 128, 8 * NO)
    WDT = abf(98, 128, 8 * D)
    ATB = [abf(16, 128, 1024), abf(18, 128, 1024)]
    ATT = [abf(20, 128, 1024), abf(22, 128, 1024)]

    XT = [afp(0, 128, 1024), afp(4, 128, 1024), afp(8, 128, 1024), afp(12, 128, 1024)]
    SQ = afp(16, 128, 1024)
    SQ2 = afp(20, 128, 1024)
    U_ = afp(0, 128, 1024)
    AU_ = afp(4, 128, 1024)
    X5 = AU_
    PQ1 = afp(8, 128, 1024)
    PQ2 = afp(12, 128, 1024)
    ISTq = afp(16, 128, 1024)
    STPq = afp(20, 128, 1024)
    TQ = afp(24, 128, 1024)
    RQ = afp(28, 128, 1024)
    XQ = [afp(32, 128, 1024), afp(36, 128, 1024)]
    XNQ = afp(40, 128, 1024)
    XNQR = [afp(40, 128, 1024), afp(48, 128, 1024)]
    ASF2 = ABF[0:64, 22528:23552]
    ZRR = {1: AFP[0:1, 13312:13824], 2: AFP[32:33, 13312:13824]}
    ZR = {1: AFP[0:1, 13824:14336], 2: AFP[32:33, 13824:14336]}
    ONE_A = AFP[0:1, 14336:14464]
    ONE_B = AFP[32:33, 14336:14464]
    ONE_C = AFP[64:65, 14336:14464]
    BIASR = AFP[64:65, 13312:14336]
    ZONE = {1: ONE_A, 2: ONE_B}
    DTp = [afp(0, 128, 1024), afp(4, 128, 1024)]
    GSp = [afp(24, 128, 1024), afp(28, 128, 1024)]
    STLp = [afp(8, 128, 1024), afp(16, 128, 1024)]
    XNQ4 = [afp(40, 128, 1024), afp(12, 128, 1024)]
    XA = [afp(32, 128, 1024), afp(36, 128, 1024)]
    XB = [afp(20, 128, 1024), afp(44, 128, 1024)]
    S1c = [STAT[:, 64:65], STAT[:, 70:71]]
    S2c = [STAT[:, 65:66], STAT[:, 71:72]]
    MVc = [STAT[:, 66:67], STAT[:, 72:73]]
    MSc = [STAT[:, 67:68], STAT[:, 73:74]]
    VVc = [STAT[:, 68:69], STAT[:, 74:75]]
    QBM = STAT[:, 80:112]
    QSC = STAT[:, 112:144]
    QIS = STAT[:, 144:176]

    with es, nc.Block() as blk:
        def wld_ap(wh):
            return wh.rearrange("(c p) j -> p c j", p=128)
        t_wq = B.dma("gpsimd", "d_wq", WLD[0].rearrange("p (c j) -> p c j", j=1024), wld_ap(wq_h))
        t_wk = B.dma("gpsimd", "d_wk", WLD[1].rearrange("p (c j) -> p c j", j=1024), wld_ap(wk_h))

        stats_tok = {}
        rstd_tok = {}
        mrs_tok = {}
        xn_tok = {}
        store_tok = {}

        s1_free = {}
        s2_free = {}

        def emit_stats(i):
            p = i % 4
            q = i % 2
            ta = B.emit("scalar", lambda s, p=p, q=q: s.activation(
                SQ[:, :], XT[p][:, :], AF.Square, accum_out=S2c[q]),
                waits=[t_ld_tok[i], s2_free.get(i - 2)], inc=True)
            tb = B.emit("scalar", lambda s, p=p, q=q: s.activation(
                SQ2[:, :], XT[p][:, :], AF.Identity, accum_out=S1c[q]),
                waits=[s1_free.get(i - 2)], inc=True)
            tv = B.emit("vector", lambda v, q=q: v.tensor_scalar(MVc[q], S1c[q], INV1024, None, ALU.mult),
                        waits=[tb, mrs_tok.get(i - 2)], inc=True)
            s1_free[i] = tv
            tv = B.emit("vector", lambda v, q=q: v.tensor_tensor(MSc[q], MVc[q], MVc[q], ALU.mult),
                        waits=[tv], inc=True)
            tv = B.emit("vector", lambda v, q=q: v.scalar_tensor_tensor(
                VVc[q], S2c[q], INV1024, MSc[q], ALU.mult, ALU.subtract),
                waits=[tv, rstd_tok.get(i - 2)], inc=True)
            s2_free[i] = tv
            tv = B.emit("vector", lambda v, q=q: v.tensor_scalar(VVc[q], VVc[q], 1e-5, None, ALU.add),
                        waits=[tv], inc=True)
            stats_tok[i] = tv

        def emit_rstd(i):
            q = i % 2
            ta = B.emit("scalar", lambda s, q=q: s.activation(VVc[q], VVc[q], AF.Ln),
                        waits=[stats_tok[i]], inc=True)
            rstd_tok[i] = B.emit("scalar", lambda s, i=i, q=q: s.activation(
                STAT[:, i:i + 1], VVc[q], AF.Exp, scale=-0.5), waits=[ta], inc=True)

        def emit_mrs(i):
            q = i % 2
            mrs_tok[i] = B.emit("vector", lambda v, i=i, q=q: v.scalar_tensor_tensor(
                STAT[:, 32 + i:33 + i], MVc[q], -1.0, STAT[:, i:i + 1], ALU.mult, ALU.mult),
                waits=[rstd_tok[i]], inc=True)

        def emit_xn(i):
            p = i % 4
            rb = STAT[:, i:i + 1].rearrange("p (b o) -> p b o", o=1).broadcast_to([128, 1, 1024])
            mb = STAT[:, 32 + i:33 + i].rearrange("p (b o) -> p b o", o=1).broadcast_to([128, 1, 1024])
            t1 = B.emit("vector", lambda v, p=p, rb=rb: v.tensor_tensor(
                XT[p][:, :].rearrange("p (b w) -> p b w", w=1024),
                XT[p][:, :].rearrange("p (b w) -> p b w", w=1024), rb, ALU.mult),
                waits=[mrs_tok[i]], inc=True)
            t2 = B.emit("vector", lambda v, i=i, p=p, mb=mb: v.tensor_tensor(
                XNBF[i % 2][:, :].rearrange("p (b w) -> p b w", w=1024),
                XT[p][:, :].rearrange("p (b w) -> p b w", w=1024), mb, ALU.add),
                waits=[t1, store_tok.get(i - 2)], inc=True)
            xn_tok[i] = t2
            store_tok[i] = B.dma("gpsimd", "d_xnw%d" % (i % 2), xn_d[i * 128:(i + 1) * 128, :],
                                 XNBF[i % 2][:, :], waits=[t2])

        t_ld_tok = {}
        t_xnt_cb = {}
        pb8 = [PSA[:, k * 512:(k + 1) * 512] for k in range(4)] + \
              [PSB[:, k * 512:(k + 1) * 512] for k in range(4)]
        ev_tok = {}
        mm_tok = {}
        kst_dma = {}
        last_ev_bank = {}
        P2S = {"g": 0, "kn": 0, "q_last": None}
        pend_evac = []

        def emit_q_group(oc, cb):
            g = P2S["g"]; P2S["g"] += 1
            w0 = [t_xnt_cb[cb], t_wq, ev_tok.get(g - 8)]
            tt = None
            for c in range(8):
                tt = B.emit("tensor", lambda t, c=c, g=g, oc=oc, cb=cb:
                            t.matmul(pb8[g % 8],
                                     WLD[0][:, c * 1024 + oc * 128: c * 1024 + (oc + 1) * 128],
                                     XNT[:, c * NT + cb * 512: c * NT + cb * 512 + 512],
                                     start=(c == 0), stop=(c == 7)),
                            waits=(w0 if c == 0 else []), inc=(c == 7))
            mm_tok[g] = tt
            P2S["q_last"] = tt

            def do_evac(g=g, oc=oc, cb=cb, tt=tt):
                ev_tok[g] = B.emit("scalar", lambda s:
                                   s.activation(QT[:, oc * NO + cb * 512: oc * NO + cb * 512 + 512],
                                                pb8[g % 8], AF.Identity),
                                   waits=[tt], inc=True)
                last_ev_bank[g % 8] = ev_tok[g]
            pend_evac.append(do_evac)

        def emit_k_group(oc, cb):
            g = P2S["g"]; P2S["g"] += 1
            kn = P2S["kn"]; P2S["kn"] += 1
            w0 = [t_xnt_cb[cb], t_wk, ev_tok.get(g - 8)]
            tt = None
            for c in range(8):
                tt = B.emit("tensor", lambda t, c=c, g=g, oc=oc, cb=cb:
                            t.matmul(pb8[g % 8],
                                     WLD[1][:, c * 1024 + oc * 128: c * 1024 + (oc + 1) * 128],
                                     XNT[:, c * NT + cb * 512: c * NT + cb * 512 + 512],
                                     start=(c == 0), stop=(c == 7)),
                            waits=(w0 if c == 0 else []), inc=(c == 7))
            mm_tok[g] = tt

            def do_evac(g=g, oc=oc, cb=cb, kn=kn, tt=tt):
                j = kn % 4
                ev_tok[g] = B.emit("scalar", lambda s:
                                   s.activation(KST[j][:, :], pb8[g % 8], AF.Identity),
                                   waits=[tt, kst_dma.get(kn - 4)], inc=True)
                last_ev_bank[g % 8] = ev_tok[g]
                kst_dma[kn] = B.dma("gpsimd", "d_ks%d" % (kn % 4),
                                    kt_d[oc * 128:(oc + 1) * 128, cb * 512:(cb + 1) * 512],
                                    KST[j][:, :], waits=[ev_tok[g]])
            pend_evac.append(do_evac)
        for i in range(NTILE + 1):
            if i < NTILE:
                t_ld_tok[i] = B.dma("sync", "d_x%d" % (i % 4), XT[i % 4][:, :],
                                    x_h[i * 128:(i + 1) * 128, :], waits=[xn_tok.get(i - 4)])
                emit_stats(i)
                emit_rstd(i)
            if i >= 1:
                emit_mrs(i - 1)
                emit_xn(i - 1)
            if i >= 4 and i % 4 == 0:
                cb = i // 4 - 1
                for fn in pend_evac:
                    fn()
                pend_evac.clear()
                tok = None
                for c in range(8):
                    tok = B.dma("sync", "d_xnt%d" % (cb % 2),
                                XNT[:, c * NT + cb * 512: c * NT + cb * 512 + 512],
                                xn_d[cb * 512:(cb + 1) * 512, c * 128:(c + 1) * 128],
                                waits=[store_tok[4 * cb + 3], store_tok[4 * cb + 2]],
                                transpose=True)
                t_xnt_cb[cb] = tok
                if cb < 4:
                    for oc in range(8):
                        emit_q_group(oc, cb)
            if i >= 6 and i % 4 == 2 and (i - 6) // 4 <= 6:
                for fn in pend_evac:
                    fn()
                pend_evac.clear()
                for oc in range(8):
                    emit_k_group(oc, (i - 6) // 4)
        for fn in pend_evac:
            fn()
        pend_evac.clear()
        for oc in range(8):
            emit_k_group(oc, 7)
        for fn in pend_evac:
            fn()
        pend_evac.clear()
        t_xnt = t_xnt_cb[7]

        t_wv = B.dma("gpsimd", "d_wv", WLD[0].rearrange("p (c j) -> p c j", j=1024), wld_ap(wv_h),
                     waits=[P2S["q_last"]])
        vst_dma = {}
        v15_mm = None
        for ti in range(32):
            for heb in range(1):
                g = P2S["g"]; P2S["g"] += 1
                vg = ti
                w0 = [t_xnt_cb[ti // 4], t_wv, ev_tok.get(g - 8)]
                tt = None
                for c in range(8):
                    tt = B.emit("tensor", lambda t, c=c, g=g, ti=ti, heb=heb:
                                t.matmul(pb8[g % 8],
                                         XNT[:, c * NT + ti * 128: c * NT + (ti + 1) * 128],
                                         WLD[0][:, c * 1024 + heb * 512: c * 1024 + (heb + 1) * 512],
                                         start=(c == 0), stop=(c == 7)),
                                waits=(w0 if c == 0 else []), inc=(c == 7))
                mm_tok[g] = tt
                if ti == 15:
                    v15_mm = tt
                j = vg % 4
                ev_tok[g] = B.emit("scalar", lambda s, g=g, j=j: s.activation(VST[j][:, :], pb8[g % 8], AF.Identity),
                                   waits=[tt, vst_dma.get(vg - 4)], inc=True)
                last_ev_bank[g % 8] = ev_tok[g]
                vst_dma[vg] = B.dma("gpsimd", "d_vs%d" % (vg % 4),
                                    v_d[ti, heb * 8:(heb + 1) * 8].rearrange("h p e -> p h e"),
                                    VST[j][:, :].rearrange("p (h e) -> p h e", e=64),
                                    waits=[ev_tok[g]])
        pe_p2_done = mm_tok[P2S["g"] - 1]
        WVX = [ABF[:, 25600 + c * 4096: 25600 + c * 4096 + 512] for c in range(8)]
        t_wvx = None
        for c in range(8):
            t_wvx = B.dma("gpsimd", "d_wv", WVX[c], wv_h[c * 128:(c + 1) * 128, 512:1024],
                          waits=([v15_mm] if c == 0 else []))
        pe_p2_cnt = B.cnt["tensor"]
        p2_stores = [(B.dma_sem(s), B.dcnt[s]) for s in
                     ("d_ks0", "d_ks1", "d_ks2", "d_ks3", "d_vs0", "d_vs1", "d_vs2", "d_vs3")]

        tva = None
        for VA in (VA1[0], VA1[1], VA2[0], VA2[1]):
            tva = B.emit("vector", lambda v, VA=VA: v.memset(
                VA.rearrange("p (c o) -> p c o", o=65)[:, :, 64:65], 1.0), waits=[pe_p2_done], inc=True)
        tva = B.emit("vector", lambda v: v.memset(ONE_A, 1.0), waits=[tva], inc=True)
        tva = B.emit("vector", lambda v: v.memset(ONE_B, 1.0), waits=[tva], inc=True)
        tva = B.emit("vector", lambda v: v.memset(ONE_C, 1.0), waits=[tva], inc=True)
        t_bias = B.dma("sync", "d_misc", BIASR, bias_h.rearrange("(o d) -> o d", o=1), waits=[tva])

        kp_tok = {}
        va_tok = {}
        for hp0 in (0, 1):
            w = p2_stores + [tva]
            kp_tok[hp0] = B.dma("sync", "d_kp%d" % hp0, KP[hp0][:, :], kt_d[hp0 * 128:(hp0 + 1) * 128, :], waits=w)
            va_tok[hp0] = []
            for (vi, (VA, h)) in enumerate(((VA1[hp0], 2 * hp0), (VA2[hp0], 2 * hp0 + 1))):
                va_tok[hp0].append(B.dma("sync", "d_va%d_%d" % (vi, hp0),
                                         VA.rearrange("p (c o) -> p c o", o=65)[:, :, 0:64],
                                         v_d[:, h].rearrange("c p e -> p c e"), waits=w))

        SPp = [PSA[:, 0:1024], PSA[:, 1024:2048]]
        ASET = [(PSB[0:65, 0:512], PSB[0:65, 512:1024])]
        pb2 = [PSB[:, 1024:1536], PSB[:, 1536:2048]]
        VSF = [AFP[:, 11264:11776], AFP[:, 11776:12288]]
        vd_state = {"n": 0, "ev": {}, "st": {}, "last_mm": None}
        vd_queue = list(range(32))

        def vd_group(ti):
            g2 = vd_state["n"]
            vd_state["n"] += 1
            w0 = [vd_state["ev"].get(g2 - 2), t_wvx, t_xnt]
            if g2 < 2:
                w0 += [last_ev_bank.get(6), last_ev_bank.get(7)]
            tt = None
            for c in range(8):
                tt = B.emit("tensor", lambda t, c=c, ti=ti, g2=g2: t.matmul(
                    pb2[g2 % 2], XNT[:, c * NT + ti * 128: c * NT + (ti + 1) * 128],
                    WVX[c], start=(c == 0), stop=(c == 7)),
                    waits=(w0 if c == 0 else []), inc=(c == 7))
            vd_state["last_mm"] = tt
            ev = B.emit("vector", lambda v, g2=g2: v.tensor_copy(VSF[g2 % 2], pb2[g2 % 2]),
                        waits=[tt, vd_state["st"].get(g2 - 2)], inc=True)
            vd_state["ev"][g2] = ev
            vd_state["st"][g2] = B.dma("gpsimd", "d_vs%d" % (g2 % 2),
                                       v_d[ti, 8:16].rearrange("h p e -> p h e"),
                                       VSF[g2 % 2].rearrange("p (h e) -> p h e", e=64),
                                       waits=[ev])
        ZBV = [(PSB[0:64, 0:512], PSB[0:64, 512:1024]),
               (PSB[0:64, 1024:1536], PSB[0:64, 1536:2048])]
        pe_hp_last = {}
        prev_exp_for_buf = [None, None]
        prev_av_for_E = [None, None, None, None]
        prev_set_tail = [None, None]
        z_prev = {1: None, 2: None}
        src_free = {}
        chainsrc = {}
        xnq_tok = {}
        chain_state = {"last": None, "prev_store": None}
        xd_store = []
        wd_store = []
        az_prev = {}
        qload_tok = {}

        def quant_chain_steps(src_ap, dst_dram, start_wait):
            toks = {}
            steps = []
            st = {"tv": None}
            w_first = [start_wait, chain_state["last"]]
            ps = chain_state["prev_store"]
            iscb = QIS.rearrange("p (b o) -> p b o", o=1).broadcast_to([128, 32, 32])
            scb = QSC.rearrange("p (b o) -> p b o", o=1).broadcast_to([128, 32, 32])

            def op(build, key=None):
                def run(build=build, key=key):
                    w = w_first if st["tv"] is None else [st["tv"]]
                    st["tv"] = B.emit("vector", build, waits=w, inc=True)
                    if key:
                        toks[key] = st["tv"]
                steps.append(run)

            op(lambda v, src_ap=src_ap: v.tensor_reduce(
                QBM, src_ap.rearrange("p (b w) -> p b w", w=32), AX.X, ALU.max,
                apply_absolute_value=True))
            op(lambda v: v.tensor_scalar(QSC, QBM, INV75, 1e-30, ALU.mult, ALU.max))
            op(lambda v: v.reciprocal(QIS, QSC))
            op(lambda v, src_ap=src_ap, iscb=iscb: v.tensor_tensor(
                U_.rearrange("p (b w) -> p b w", w=32),
                src_ap.rearrange("p (b w) -> p b w", w=32), iscb, ALU.mult), key="src_free")
            op(lambda v: v.tensor_scalar(U_, U_, 7.5, -7.5, ALU.min, ALU.max))
            op(lambda v: v.tensor_scalar(PQ1, U_, -1.0, None, ALU.mult))
            op(lambda v: v.tensor_tensor(AU_, U_, PQ1, ALU.max))
            op(lambda v: v.tensor_scalar(PQ1, AU_, 2.0, -4.0, ALU.is_ge, ALU.mult))
            op(lambda v: v.tensor_scalar(PQ2, AU_, 4.0, -2.0, ALU.is_ge, ALU.mult))
            op(lambda v: v.scalar_tensor_tensor(ISTq, PQ1, 8.0, PQ2, ALU.add, ALU.add))
            op(lambda v: v.tensor_scalar(PQ1, AU_, 2.0, 0.125, ALU.is_ge, ALU.mult))
            op(lambda v: v.tensor_scalar(PQ2, AU_, 4.0, 0.25, ALU.is_ge, ALU.mult))
            op(lambda v: v.scalar_tensor_tensor(STPq, PQ1, 0.125, PQ2, ALU.add, ALU.add))
            op(lambda v: v.tensor_tensor(TQ, AU_, ISTq, ALU.mult))
            op(lambda v: v.tensor_scalar(X5, TQ, 0.5, None, ALU.add))
            for wgt in (16.0, 8.0, 4.0, 2.0, 1.0):
                op(lambda v, wgt=wgt: v.tensor_scalar(PQ1, X5, wgt, -wgt, ALU.is_ge, ALU.mult))
                op(lambda v: v.tensor_tensor(X5, X5, PQ1, ALU.add))
            op(lambda v: v.tensor_scalar(RQ, TQ, 0.5, None, ALU.add))
            op(lambda v: v.tensor_tensor(RQ, RQ, X5, ALU.subtract))
            op(lambda v: v.tensor_tensor(RQ, RQ, STPq, ALU.mult))
            op(lambda v: v.tensor_scalar(PQ1, U_, 0.0, 2.0, ALU.is_ge, ALU.mult))
            op(lambda v: v.tensor_tensor(PQ2, RQ, PQ1, ALU.mult))
            op(lambda v: v.tensor_tensor(RQ, PQ2, RQ, ALU.subtract))

            def run_final():
                tv = B.emit("vector", lambda v, scb=scb: v.tensor_tensor(
                    XDBF.rearrange("p (b w) -> p b w", w=32),
                    RQ.rearrange("p (b w) -> p b w", w=32), scb, ALU.mult),
                    waits=[st["tv"], ps], inc=True)
                toks["last"] = tv
                chain_state["last"] = tv
                toks["store"] = B.dma("gpsimd", "d_xd", dst_dram, XDBF, waits=[tv])
                chain_state["prev_store"] = toks["store"]
            steps.append(run_final)
            return toks, steps

        def task_load(u):
            if u < 16:
                qload_tok[u] = B.dma("sync", "d_xq%d" % (u % 2), XQ[u % 2][:, :],
                                     x_h[u * 128:(u + 1) * 128, :], waits=[src_free.get(u - 2)])
            elif u < 24:
                j = u - 16
                qload_tok[u] = B.dma("sync", "d_xq%d" % (u % 2), XQ[u % 2][:, :],
                                     wf_h[j * 128:(j + 1) * 128, :], waits=[src_free.get(u - 2)])

        task_load(0)
        for u in range(32):
            hp, qgi = divmod(u, 4)
            h1, h2 = 2 * hp, 2 * hp + 1
            bufK = hp % 2
            aset = 0
            A1, A2 = ASET[aset]
            if qgi == 0 and 1 <= hp <= 6:
                nb = (hp + 1) % 2
                w = [pe_hp_last[hp - 1]]
                kp_tok[hp + 1] = B.dma("sync", "d_kp%d" % nb, KP[nb][:, :],
                                       kt_d[(hp + 1) * 128:(hp + 2) * 128, :], waits=w)
                va_tok[hp + 1] = []
                wv2 = list(w)
                if hp + 1 >= 4:
                    wv2 += [vd_state["st"][31], vd_state["st"][30]]
                for (vi, (VA, h)) in enumerate(((VA1[nb], 2 * (hp + 1)), (VA2[nb], 2 * (hp + 1) + 1))):
                    va_tok[hp + 1].append(B.dma("sync", "d_va%d_%d" % (vi, nb),
                                                VA.rearrange("p (c o) -> p c o", o=65)[:, :, 0:64],
                                                v_d[:, h].rearrange("c p e -> p c e"), waits=wv2))
            unit_ready = [kp_tok[hp]] + va_tok[hp]
            task_load(u + 1)

            if u < 16:
                t_xnq = B.emit("scalar", lambda s, u=u: s.activation(
                    XNQR[u % 2][:, :], XQ[u % 2][:, :], AF.Identity,
                    scale=STAT[:, u:u + 1], bias=STAT[:, 32 + u:33 + u]),
                    waits=[qload_tok[u], chainsrc.get(u - 2)], inc=True)
                xnq_tok[u] = t_xnq
            chain_steps = None
            ctoks = None
            if u < 16:
                ctoks, chain_steps = quant_chain_steps(
                    XNQR[u % 2][:, :], xd_d[u * 128:(u + 1) * 128, :], xnq_tok[u])
                src_free[u] = xnq_tok[u]
            elif u < 24:
                j2 = u - 16
                ctoks, chain_steps = quant_chain_steps(
                    XQ[u % 2][:, :], wd_d[j2 * 128:(j2 + 1) * 128, :], qload_tok[u])
            if u == 13:
                p3_wp = B.dma("gpsimd", "d_wp", WPB.rearrange("p (c j) -> p c j", j=1024),
                              wld_ap(wp_h), waits=[vd_state["last_mm"]])
            if u == 17:
                p3_xdt = None
                for ic in range(8):
                    p3_xdt = B.dma("sync", "d_xdt2", XDT[:, ic * NO:(ic + 1) * NO],
                                   xd_d[:, ic * 128:(ic + 1) * 128],
                                   waits=[xd_store[-1], vd_state["last_mm"] if ic == 0 else None],
                                   transpose=True)
            if u == 25:
                p3_wdt = None
                for ic in range(8):
                    p3_wdt = B.dma("sync", "d_wdt2", WDT[:, ic * D:(ic + 1) * D],
                                   wd_d[:, ic * 128:(ic + 1) * 128],
                                   waits=[wd_store[-1], vd_state["last_mm"] if ic == 0 else None],
                                   transpose=True)

            qcol = hp * NO + qgi * 512
            exp_tok = {}
            sc_tok = {}
            av_list = {}

            def do_av(t2, A1=A1, A2=A2, bufK=bufK, aset=aset):
                wav = [exp_tok[t2]]
                if t2 == 0:
                    wav.append(prev_set_tail[aset])
                    if u == 0:
                        wav += [last_ev_bank.get(4), last_ev_bank.get(5)]
                B.emit("tensor", lambda t, t2=t2, A1=A1, bufK=bufK: t.matmul(
                    A1, VA1[bufK][:, t2 * 65:t2 * 65 + 65],
                    EB4[t2 % 4][:, 0:512],
                    start=(t2 == 0), stop=(t2 == 31)), waits=wav, inc=False)
                av_list[t2] = B.emit("tensor", lambda t, t2=t2, A2=A2, bufK=bufK: t.matmul(
                    A2, VA2[bufK][:, t2 * 65:t2 * 65 + 65],
                    EB4[t2 % 4][:, 512:1024],
                    start=(t2 == 0), stop=(t2 == 31)), waits=[], inc=True)

            for tc in range(32):
                b = tc % 2
                if tc < 2:
                    w_sc = [prev_exp_for_buf[b]]
                    if tc == 0:
                        w_sc += unit_ready
                    if u == 0:
                        w_sc += [last_ev_bank.get(2 * b), last_ev_bank.get(2 * b + 1)]
                else:
                    w_sc = [exp_tok[tc - 2]]
                B.emit("tensor", lambda t, tc=tc, b=b, bufK=bufK, qcol=qcol: t.matmul(
                    SPp[b][:, 0:512],
                    KP[bufK][0:64, tc * 128:(tc + 1) * 128],
                    QT[0:64, qcol:qcol + 512],
                    start=True, stop=True, tile_position=(0, 0)), waits=w_sc, inc=False)
                sc_tok[tc] = B.emit("tensor", lambda t, tc=tc, b=b, bufK=bufK, qcol=qcol: t.matmul(
                    SPp[b][:, 512:1024],
                    KP[bufK][64:128, tc * 128:(tc + 1) * 128],
                    QT[64:128, qcol:qcol + 512],
                    start=True, stop=True, tile_position=(64, 0)), waits=[], inc=True)
                we = [sc_tok[tc]]
                if tc >= 4:
                    we.append(av_list[tc - 4])
                else:
                    we.append(prev_av_for_E[tc % 4])
                exp_tok[tc] = B.emit("scalar", lambda s, b=b, tc=tc: s.activation(
                    EB4[tc % 4][:, 0:1024], SPp[b][:, :], AF.Exp, scale=0.125), waits=we, inc=True)
                if tc >= 2:
                    do_av(tc - 2)
                if u <= 11 and tc in (6, 16, 26) and vd_queue:
                    vd_group(vd_queue.pop(0))
                if chain_steps and tc < 27:
                    chain_steps.pop(0)()
            do_av(30)
            do_av(31)
            prev_exp_for_buf = [exp_tok[30], exp_tok[31]]
            prev_av_for_E = [av_list[28], av_list[29], av_list[30], av_list[31]]
            if qgi == 3:
                pe_hp_last[hp] = av_list[31]

            tva_ = B.emit("vector", lambda v: v.tensor_copy(ASF2[:, :], PSB[0:64, 0:1024]),
                          waits=[av_list[31], az_prev.get(1), az_prev.get(2)], inc=True)
            tz = {}
            for hs, Abank, hh in ((1, A1, h1), (2, A2, h2)):
                tz[hs] = B.emit("vector", lambda v, hs=hs, Abank=Abank: v.tensor_copy(
                    ZR[hs], Abank[64:65, :]), waits=[tva_ if hs == 1 else tz[1], z_prev.get(hs)], inc=True)
                az_prev[hs] = B.dma("gpsimd", "d_az%d" % hs,
                                    a_d[qgi * 4:(qgi + 1) * 4, hh].rearrange("b p j -> p b j"),
                                    ASF2[:, (hs - 1) * 512:hs * 512].rearrange("p (b j) -> p b j", j=128),
                                    waits=[tva_])
                z_prev[hs] = B.dma("gpsimd", "d_z%d" % hs,
                                   z_d[hh:hh + 1, qgi * 512:(qgi + 1) * 512], ZR[hs], waits=[tz[hs]])
            prev_set_tail[aset] = tz[2]
            while chain_steps:
                chain_steps.pop(0)()

            if u < 16:
                chainsrc[u] = ctoks["src_free"]
                xd_store.append(ctoks["store"])
            elif u < 24:
                src_free[u] = ctoks["src_free"]
                wd_store.append(ctoks["store"])
        last_chain = chain_state["last"]

        t_wp = p3_wp
        ZV = afp(24, 128, 256)
        ZRV = afp(25, 128, 256)
        t_zv = B.dma("sync", "d_zv", ZV, z_d.rearrange("h (a q) -> (h a) q", a=8),
                     waits=[z_prev[1], z_prev[2], last_chain])
        t_zr = B.emit("vector", lambda v: v.reciprocal(ZRV, ZV), waits=[t_zv], inc=True)
        t_zrs = B.dma("gpsimd", "d_zr", zr_d.rearrange("h (a q) -> (h a) q", a=8), ZRV, waits=[t_zr])
        t_xdt = p3_xdt
        t_wdt = p3_wdt

        PPq = [[PSA[:, 0:512], PSA[:, 512:1024]], [PSA[:, 1024:1536], PSA[:, 1536:2048]]]
        PFq = PSB[:, 0:1024]
        DP = PSB[:, 1024:2048]
        ZPK2 = [AFP[0:2, 13312:14336], AFP[0:2, 14464:15488]]
        ONESEL = AFP[0:2, 12288:12416]
        ROWB = AFP[0:1, 12416:12544]
        t_osel = B.emit("vector", lambda v: v.memset(ONESEL[0:1, 0:64], 1.0),
                        waits=[t_zrs, last_chain], inc=True)
        t_osel = B.emit("vector", lambda v: v.memset(ONESEL[0:1, 64:128], 0.0),
                        waits=[t_osel], inc=True)
        t_osel = B.emit("vector", lambda v: v.memset(ROWB[0:1, 0:64], 0.0),
                        waits=[t_osel], inc=True)
        t_osel = B.emit("vector", lambda v: v.memset(ROWB[0:1, 64:128], 1.0),
                        waits=[t_osel], inc=True)
        t_osel = B.dma("sync", "d_os", ONESEL[1:2, :], ROWB[0:1, :], waits=[t_osel])
        dmm_tok = {}
        dmul_tok = {}
        xn4_tok = {}
        pmm_tok = {}
        sadd_tok = {}
        gelu_tok = {}
        ffn_tok = {}
        silu_tok = {}
        sg_tok = {}
        outtok = {}
        out_store = {}
        xb_tok = {}

        def p4_stage1(qt):
            pr = qt % 2
            w_z = [dmm_tok.get(qt - 2),
                   t_zrs if qt <= 1 else None,
                   az_prev[1] if qt <= 1 else None,
                   az_prev[2] if qt <= 1 else None]
            tz = None
            for pg in range(2):
                dst = ZPK2[pr][pg:pg + 1, :].rearrange("o (c j) -> o c j", j=128)
                srcz = zr_d[pg::2, qt * 128:(qt + 1) * 128].rearrange("(o c) j -> o c j", o=1)
                tz = B.dma("sync", "d_dt%d" % pr, dst, srcz, waits=(w_z if pg == 0 else []))
            t_x4 = B.dma("sync", "d_p4x%d" % pr, XA[pr][:, :], x_h[qt * 128:(qt + 1) * 128, :],
                         waits=[xn4_tok.get(qt - 2),
                                last_chain if qt <= 1 else None,
                                xnq_tok[15] if qt <= 1 else None])
            w_at = [dmul_tok.get(qt - 2), t_wp,
                    t_zrs if qt <= 1 else None,
                    wd_store[-1] if qt == 0 else None,
                    az_prev[1] if qt == 0 else None,
                    az_prev[2] if qt == 0 else None,
                    pe_hp_last[7] if qt == 0 else None]
            t_at_e = B.dma("sync", "d_p4a%d" % pr,
                           ATB[pr][0:64, :].rearrange("p (c j) -> p c j", j=128),
                           a_d[qt].rearrange("(c t) p j -> t c p j", t=2)[0].rearrange("c p j -> p c j"),
                           waits=w_at)
            t_at_o = B.dma("sync", "d_p4a%d" % pr,
                           ATB[pr][64:128, :].rearrange("p (c j) -> p c j", j=128),
                           a_d[qt].rearrange("(c t) p j -> t c p j", t=2)[1].rearrange("c p j -> p c j"),
                           waits=[])
            if qt >= 1:
                pr2 = (qt - 1) % 2
                xb_tok[qt - 1] = B.dma("sync", "d_p4b%d" % pr2, XB[pr2][:, :],
                                       x_h[(qt - 1) * 128:qt * 128, :],
                                       waits=[outtok.get(qt - 3),
                                              last_chain if qt - 1 <= 1 else None])
            w_dm = [tz, dmul_tok.get(qt - 1), t_osel if qt <= 1 else None,
                    vd_state["ev"][31] if qt == 0 else None]
            tmm = None
            for nh in range(2):
                tmm = B.emit("tensor", lambda t, nh=nh, pr=pr:
                             t.matmul(DP[:, nh * 512:(nh + 1) * 512],
                                      ONESEL[:, :], ZPK2[pr][:, nh * 512:(nh + 1) * 512],
                                      start=True, stop=True),
                             waits=(w_dm if nh == 0 else []), inc=(nh == 1))
            dmm_tok[qt] = tmm
            t_xn4 = B.emit("scalar", lambda s, qt=qt, pr=pr: s.activation(
                XNQ4[qt % 2][:, :], XA[pr][:, :], AF.Identity,
                scale=STAT[:, qt:qt + 1], bias=STAT[:, 32 + qt:33 + qt]),
                waits=[t_x4, sadd_tok.get(qt - 2),
                       az_prev[1] if qt <= 1 else None,
                       az_prev[2] if qt <= 1 else None], inc=True)
            xn4_tok[qt] = t_xn4
            t_dm = B.emit("vector", lambda v, pr=pr: v.tensor_tensor(
                DTp[pr][:, :], ATB[pr][:, :], DP[:, :], ALU.mult),
                waits=[t_at_o, dmm_tok[qt], gelu_tok.get(qt - 2)], inc=True)
            dmul_tok[qt] = t_dm

        def p4_stage2(qt):
            pr = qt % 2
            t_gelu = B.emit("scalar", lambda s, qt=qt, pr=pr: s.activation(
                ATT[pr][:, :], DTp[pr][:, :], AF.Gelu),
                waits=[dmul_tok[qt], pmm_tok.get(qt - 2)], inc=True)
            gelu_tok[qt] = t_gelu
            ts_list = []
            for half in range(2):
                w0 = [t_gelu, sadd_tok.get(qt - 2)]
                tt = None
                for hc in range(8):
                    tt = B.emit("tensor", lambda t, hc=hc, half=half, qt=qt, pr=pr:
                                t.matmul(PPq[pr][half], ATT[pr][:, hc * 128:(hc + 1) * 128],
                                         WPB[:, hc * 1024 + half * 512: hc * 1024 + half * 512 + 512],
                                         start=(hc == 0), stop=(hc == 7)),
                                waits=(w0 if hc == 0 else []), inc=(hc == 7))
                if half == 1:
                    pmm_tok[qt] = tt
                ts_ = B.emit("vector", lambda v, half=half, pr=pr, qt=qt: v.tensor_tensor(
                    STLp[pr][:, half * 512:(half + 1) * 512], PPq[pr][half],
                    XNQ4[qt % 2][:, half * 512:(half + 1) * 512],
                    ALU.add), waits=[tt, xn4_tok[qt], sg_tok.get(qt - 2)], inc=True)
                ts_list.append(ts_)
            sadd_tok[qt] = ts_list[-1]

        def emit_ffn(qt):
            w0 = [t_xdt, t_wdt, silu_tok.get(qt - 1)]
            if qt == 0:
                w0.append(prev_set_tail[0])
                w0.append(t_bias)
            tt = None
            for half in range(2):
                for ic in range(8):
                    tt = B.emit("tensor", lambda t, ic=ic, half=half, qt=qt:
                                t.matmul(PFq[:, half * 512:(half + 1) * 512],
                                         XDT[:, ic * NO + qt * 128: ic * NO + (qt + 1) * 128],
                                         WDT[:, ic * 1024 + half * 512: ic * 1024 + half * 512 + 512],
                                         start=(ic == 0), stop=False),
                                waits=(w0 if (half == 0 and ic == 0) else []), inc=False)
                tt = B.emit("tensor", lambda t, half=half:
                            t.matmul(PFq[:, half * 512:(half + 1) * 512],
                                     ONE_C[:, :], BIASR[:, half * 512:half * 512 + 512],
                                     start=False, stop=True),
                            waits=[], inc=(half == 1))
            ffn_tok[qt] = tt

        def p4_stage3(qt):
            pr = qt % 2
            t_silu = B.emit("scalar", lambda s, pr=pr: s.activation(GSp[pr][:, :], PFq[:, :], AF.Silu),
                            waits=[ffn_tok[qt], out_store.get(qt - 2)], inc=True)
            silu_tok[qt] = t_silu
            if qt + 1 < OTILE:
                emit_ffn(qt + 1)
            tv = B.emit("vector", lambda v, pr=pr: v.tensor_tensor(
                GSp[pr][:, :], GSp[pr][:, :], STLp[pr][:, :], ALU.mult),
                waits=[t_silu, sadd_tok[qt]], inc=True)
            sg_tok[qt] = tv
            tv = B.emit("vector", lambda v, qt=qt, pr=pr: v.tensor_tensor(
                GSp[pr][:, :], GSp[pr][:, :], XB[pr][:, :], ALU.add),
                waits=[tv, xb_tok[qt]], inc=True)
            outtok[qt] = tv
            out_store[qt] = B.dma("gpsimd", "d_out%d" % pr, out_h[qt * 128:(qt + 1) * 128, :],
                                  GSp[pr][:, :], waits=[tv])

        p4_stage1(0)
        p4_stage1(1)
        emit_ffn(0)
        for qt in range(OTILE):
            p4_stage2(qt)
            if qt >= 1:
                p4_stage3(qt - 1)
            if qt + 2 < OTILE:
                p4_stage1(qt + 2)
        qlast = OTILE - 1
        xb_tok[qlast] = B.dma("sync", "d_p4b%d" % (qlast % 2), XB[qlast % 2][:, :],
                              x_h[qlast * 128:(qlast + 1) * 128, :],
                              waits=[outtok.get(qlast - 2)])
        p4_stage3(qlast)
        B.wait("gpsimd", [out_store[OTILE - 1], out_store[OTILE - 2]])
        B.wait("sync", [out_store[OTILE - 1], out_store[OTILE - 2]])

        @blk.sync
        def _(sync):
            for fn in B.ops["sync"]:
                fn(sync)

        @blk.gpsimd
        def _(g):
            for fn in B.ops["gpsimd"]:
                fn(g)

        @blk.vector
        def _(v):
            for fn in B.ops["vector"]:
                fn(v)

        @blk.scalar
        def _(s):
            for fn in B.ops["scalar"]:
                fn(s)

        @blk.tensor
        def _(t):
            for fn in B.ops["tensor"]:
                fn(t)

    return nc


_NC_CACHE = []


def kernel(L0_attn_X, L0_attn_Wq, L0_attn_Wk, L0_attn_Wv, L0_attn_Wpost,
           L0_ffn_weight, L0_ffn_bias):
    x = np.ascontiguousarray(np.asarray(L0_attn_X, dtype=np.float32))
    wq = np.ascontiguousarray(np.asarray(L0_attn_Wq, dtype=np.float32))
    wk = np.ascontiguousarray(np.asarray(L0_attn_Wk, dtype=np.float32))
    wv = np.ascontiguousarray(np.asarray(L0_attn_Wv, dtype=np.float32))
    wp = np.ascontiguousarray(np.asarray(L0_attn_Wpost, dtype=np.float32))
    wf = np.ascontiguousarray(np.asarray(L0_ffn_weight, dtype=np.float32))
    bias = np.ascontiguousarray(np.asarray(L0_ffn_bias, dtype=np.float32))

    if not _NC_CACHE:
        _NC_CACHE.append(build_program())
    nc = _NC_CACHE[0]

    in_maps = []
    for c in range(8):
        b, h = divmod(c, 2)
        own = x[b, h * 2048:(h + 1) * 2048]
        oth = x[b, (1 - h) * 2048:(2 - h) * 2048]
        xc = np.ascontiguousarray(np.concatenate([own, oth], axis=0))
        in_maps.append({"x": xc, "wq": wq, "wk": wk, "wv": wv,
                        "wpost": wp, "w": wf, "bias": bias})
    res = run_bass_kernel_spmd(nc, in_maps, core_ids=list(range(8)))
    out = np.empty((4, 4096, 1024), dtype=np.float32)
    for c in range(8):
        b, h = divmod(c, 2)
        out[b, h * 2048:(h + 1) * 2048] = res.results[c]["out"]
    return out


# version 26
# speedup vs baseline: 1.0893x; 1.0066x over previous
import numpy as np
import concourse.bass as bass
import concourse.mybir as mybir
from concourse.bass_utils import run_bass_kernel_spmd

F32 = mybir.dt.float32
BF16 = mybir.dt.bfloat16
AF = mybir.ActivationFunctionType
ALU = mybir.AluOpType
AX = mybir.AxisListType

NT = 4096
NO = 2048
D = 1024
NTILE = 32
OTILE = 16
INV75 = float(np.float32(1.0 / 7.5))
INV1024 = float(np.float32(1.0 / 1024.0))


class Builder:
    ENGINES = ("sync", "gpsimd", "vector", "scalar", "tensor")

    def __init__(self, nc, es):
        self.nc = nc
        self.es = es
        self.ops = {e: [] for e in self.ENGINES}
        self.sem = {}
        self.cnt = {}
        for e in ("vector", "scalar", "tensor"):
            self.sem[e] = es.enter_context(nc.semaphore("sem_" + e))
            self.cnt[e] = 0
        self.dsem = {}
        self.dcnt = {}

    def dma_sem(self, name):
        if name not in self.dsem:
            self.dsem[name] = self.es.enter_context(self.nc.semaphore(name))
            self.dcnt[name] = 0
        return self.dsem[name]

    def emit(self, eng, fn, waits=(), inc=False):
        w = [t for t in waits if t is not None]
        tok = None
        sem = None
        if inc:
            self.cnt[eng] += 1
            sem = self.sem[eng]
            tok = (sem, self.cnt[eng])

        def run(engine, w=w, fn=fn, sem=sem):
            for (s, v) in w:
                engine.wait_ge(s, v)
            instr = fn(engine)
            if sem is not None:
                instr.then_inc(sem, 1)
        self.ops[eng].append(run)
        return tok

    def dma(self, queue, stream, out, in_, waits=(), **kw):
        sem = self.dma_sem(stream)
        self.dcnt[stream] += 16
        tok = (sem, self.dcnt[stream])
        w = [t for t in waits if t is not None]

        def run(engine, w=w, out=out, in_=in_, sem=sem, kw=kw):
            for (s, v) in w:
                engine.wait_ge(s, v)
            engine.dma_start(out=out, in_=in_, **kw).then_inc(sem, 16)
        self.ops[queue].append(run)
        return tok

    def wait(self, eng, waits):
        w = [t for t in waits if t is not None]

        def run(engine, w=w):
            for (s, v) in w:
                engine.wait_ge(s, v)
        self.ops[eng].append(run)


def build_program():
    nc = bass.Bass("TRN2", target_bir_lowering=False)
    x_h = nc.dram_tensor("x", [NT, D], F32, kind="ExternalInput").ap()
    wq_h = nc.dram_tensor("wq", [D, D], F32, kind="ExternalInput").ap()
    wk_h = nc.dram_tensor("wk", [D, D], F32, kind="ExternalInput").ap()
    wv_h = nc.dram_tensor("wv", [D, D], F32, kind="ExternalInput").ap()
    wp_h = nc.dram_tensor("wpost", [D, D], F32, kind="ExternalInput").ap()
    wf_h = nc.dram_tensor("w", [D, D], F32, kind="ExternalInput").ap()
    bias_h = nc.dram_tensor("bias", [D], F32, kind="ExternalInput").ap()
    out_h = nc.dram_tensor("out", [NO, D], F32, kind="ExternalOutput").ap()
    xn_d = nc.dram_tensor("xn_d", [NT, D], BF16, kind="Internal").ap()
    kt_d = nc.dram_tensor("kt_d", [D, NT], BF16, kind="Internal").ap()
    v_d = nc.dram_tensor("v_d", [NTILE, 16, 128, 64], BF16, kind="Internal").ap()
    a_d = nc.dram_tensor("a_d", [OTILE, 16, 64, 128], BF16, kind="Internal").ap()
    xd_d = nc.dram_tensor("xd_d", [NO, D], BF16, kind="Internal").ap()
    wd_d = nc.dram_tensor("wd_d", [D, D], BF16, kind="Internal").ap()
    s_dd = nc.dram_tensor("s_dd", [NO, D], F32, kind="Internal").ap()
    z_d = nc.dram_tensor("z_d", [16, NO], F32, kind="Internal").ap()
    zr_d = nc.dram_tensor("zr_d", [16, NO], F32, kind="Internal").ap()

    from contextlib import ExitStack
    es = ExitStack()
    B = Builder(nc, es)

    ABF = es.enter_context(nc.sbuf_tensor("ABF", [128, 74752], BF16))
    AFP = es.enter_context(nc.sbuf_tensor("AFP", [128, 15488], F32))
    STAT = es.enter_context(nc.sbuf_tensor("STAT", [128, 176], F32))
    PSA = es.enter_context(nc.psum_tensor("PSA", [128, 2048], F32))
    PSB = es.enter_context(nc.psum_tensor("PSB", [128, 2048], F32))

    def abf(kb, p, n):
        return ABF[0:p, kb * 512: kb * 512 + n]

    def afp(kb, p, n):
        return AFP[0:p, kb * 256: kb * 256 + n]

    XNT = abf(50, 128, 8 * NT)
    QT = abf(114, 128, 8 * NO)
    WLD = [abf(0, 128, 8192), abf(16, 128, 8192)]
    XNBF = [abf(32, 128, 1024), abf(34, 128, 1024)]
    KST = [abf(36, 128, 512), abf(37, 128, 512), abf(42, 128, 512), abf(43, 128, 512)]
    VST = [abf(38, 128, 512), abf(39, 128, 512), abf(40, 128, 512), abf(41, 128, 512)]
    KP = [abf(0, 128, 4096), abf(8, 128, 4096)]
    VA1 = [abf(16, 128, 2080), abf(21, 128, 2080)]
    VA2 = [abf(26, 128, 2080), abf(31, 128, 2080)]
    EB = [abf(36, 128, 2048), abf(40, 128, 2048)]
    EB4 = [abf(36, 128, 1024), abf(38, 128, 1024), abf(40, 128, 1024), abf(42, 128, 1024)]
    XDBF = abf(48, 128, 1024)
    WPB = abf(50, 128, 8 * D)
    XDT = abf(66, 128, 8 * NO)
    WDT = abf(98, 128, 8 * D)
    ATB = [abf(16, 128, 1024), abf(18, 128, 1024)]
    ATT = [abf(20, 128, 1024), abf(22, 128, 1024)]

    XT = [afp(0, 128, 1024), afp(4, 128, 1024), afp(8, 128, 1024), afp(12, 128, 1024)]
    SQ = afp(16, 128, 1024)
    SQ2 = afp(20, 128, 1024)
    U_ = afp(0, 128, 1024)
    AU_ = afp(4, 128, 1024)
    X5 = AU_
    PQ1 = afp(8, 128, 1024)
    PQ2 = afp(12, 128, 1024)
    ISTq = afp(16, 128, 1024)
    STPq = afp(20, 128, 1024)
    TQ = afp(24, 128, 1024)
    RQ = afp(28, 128, 1024)
    XQ = [afp(32, 128, 1024), afp(36, 128, 1024)]
    XNQ = afp(40, 128, 1024)
    XNQR = [afp(40, 128, 1024), afp(48, 128, 1024)]
    ASF2 = ABF[0:64, 22528:23552]
    ZRR = {1: AFP[0:1, 13312:13824], 2: AFP[32:33, 13312:13824]}
    ZR = {1: AFP[0:1, 13824:14336], 2: AFP[32:33, 13824:14336]}
    ONE_A = AFP[0:1, 14336:14464]
    ONE_B = AFP[32:33, 14336:14464]
    ONE_C = AFP[64:65, 14336:14464]
    BIASR = AFP[64:65, 13312:14336]
    ZONE = {1: ONE_A, 2: ONE_B}
    DTp = [afp(0, 128, 1024), afp(4, 128, 1024)]
    GSp = [afp(24, 128, 1024), afp(28, 128, 1024)]
    STLp = [afp(8, 128, 1024), afp(16, 128, 1024)]
    XNQ4 = [afp(40, 128, 1024), afp(12, 128, 1024)]
    XA = [afp(32, 128, 1024), afp(36, 128, 1024)]
    XB = [afp(20, 128, 1024), afp(44, 128, 1024)]
    S1c = [STAT[:, 64:65], STAT[:, 70:71]]
    S2c = [STAT[:, 65:66], STAT[:, 71:72]]
    MVc = [STAT[:, 66:67], STAT[:, 72:73]]
    MSc = [STAT[:, 67:68], STAT[:, 73:74]]
    VVc = [STAT[:, 68:69], STAT[:, 74:75]]
    QBM = STAT[:, 80:112]
    QSC = STAT[:, 112:144]
    QIS = STAT[:, 144:176]

    with es, nc.Block() as blk:
        def wld_ap(wh):
            return wh.rearrange("(c p) j -> p c j", p=128)
        t_wq = B.dma("gpsimd", "d_wq", WLD[0].rearrange("p (c j) -> p c j", j=1024), wld_ap(wq_h))
        t_wk = B.dma("gpsimd", "d_wk", WLD[1].rearrange("p (c j) -> p c j", j=1024), wld_ap(wk_h))

        stats_tok = {}
        rstd_tok = {}
        mrs_tok = {}
        xn_tok = {}
        store_tok = {}

        s1_free = {}
        s2_free = {}

        def emit_stats(i):
            p = i % 4
            q = i % 2
            ta = B.emit("scalar", lambda s, p=p, q=q: s.activation(
                SQ[:, :], XT[p][:, :], AF.Square, accum_out=S2c[q]),
                waits=[t_ld_tok[i], s2_free.get(i - 2)], inc=True)
            tb = B.emit("scalar", lambda s, p=p, q=q: s.activation(
                SQ2[:, :], XT[p][:, :], AF.Identity, accum_out=S1c[q]),
                waits=[s1_free.get(i - 2)], inc=True)
            tv = B.emit("vector", lambda v, q=q: v.tensor_scalar(MVc[q], S1c[q], INV1024, None, ALU.mult),
                        waits=[tb, mrs_tok.get(i - 2)], inc=True)
            s1_free[i] = tv
            tv = B.emit("vector", lambda v, q=q: v.tensor_tensor(MSc[q], MVc[q], MVc[q], ALU.mult),
                        waits=[tv], inc=True)
            tv = B.emit("vector", lambda v, q=q: v.scalar_tensor_tensor(
                VVc[q], S2c[q], INV1024, MSc[q], ALU.mult, ALU.subtract),
                waits=[tv, rstd_tok.get(i - 2)], inc=True)
            s2_free[i] = tv
            tv = B.emit("vector", lambda v, q=q: v.tensor_scalar(VVc[q], VVc[q], 1e-5, None, ALU.add),
                        waits=[tv], inc=True)
            stats_tok[i] = tv

        def emit_rstd(i):
            q = i % 2
            ta = B.emit("scalar", lambda s, q=q: s.activation(VVc[q], VVc[q], AF.Ln),
                        waits=[stats_tok[i]], inc=True)
            rstd_tok[i] = B.emit("scalar", lambda s, i=i, q=q: s.activation(
                STAT[:, i:i + 1], VVc[q], AF.Exp, scale=-0.5), waits=[ta], inc=True)

        def emit_mrs(i):
            q = i % 2
            mrs_tok[i] = B.emit("vector", lambda v, i=i, q=q: v.scalar_tensor_tensor(
                STAT[:, 32 + i:33 + i], MVc[q], -1.0, STAT[:, i:i + 1], ALU.mult, ALU.mult),
                waits=[rstd_tok[i]], inc=True)

        def emit_xn(i):
            p = i % 4
            rb = STAT[:, i:i + 1].rearrange("p (b o) -> p b o", o=1).broadcast_to([128, 1, 1024])
            mb = STAT[:, 32 + i:33 + i].rearrange("p (b o) -> p b o", o=1).broadcast_to([128, 1, 1024])
            t1 = B.emit("vector", lambda v, p=p, rb=rb: v.tensor_tensor(
                XT[p][:, :].rearrange("p (b w) -> p b w", w=1024),
                XT[p][:, :].rearrange("p (b w) -> p b w", w=1024), rb, ALU.mult),
                waits=[mrs_tok[i]], inc=True)
            t2 = B.emit("vector", lambda v, i=i, p=p, mb=mb: v.tensor_tensor(
                XNBF[i % 2][:, :].rearrange("p (b w) -> p b w", w=1024),
                XT[p][:, :].rearrange("p (b w) -> p b w", w=1024), mb, ALU.add),
                waits=[t1, store_tok.get(i - 2)], inc=True)
            xn_tok[i] = t2
            store_tok[i] = B.dma("gpsimd", "d_xnw%d" % (i % 2), xn_d[i * 128:(i + 1) * 128, :],
                                 XNBF[i % 2][:, :], waits=[t2])

        t_ld_tok = {}
        t_xnt_cb = {}
        pb8 = [PSA[:, k * 512:(k + 1) * 512] for k in range(4)] + \
              [PSB[:, k * 512:(k + 1) * 512] for k in range(4)]
        ev_tok = {}
        mm_tok = {}
        kst_dma = {}
        last_ev_bank = {}
        P2S = {"g": 0, "kn": 0, "q_last": None}
        pend_evac = []

        def emit_q_group(oc, cb):
            g = P2S["g"]; P2S["g"] += 1
            w0 = [t_xnt_cb[cb], t_wq, ev_tok.get(g - 8)]
            tt = None
            for c in range(8):
                tt = B.emit("tensor", lambda t, c=c, g=g, oc=oc, cb=cb:
                            t.matmul(pb8[g % 8],
                                     WLD[0][:, c * 1024 + oc * 128: c * 1024 + (oc + 1) * 128],
                                     XNT[:, c * NT + cb * 512: c * NT + cb * 512 + 512],
                                     start=(c == 0), stop=(c == 7)),
                            waits=(w0 if c == 0 else []), inc=(c == 7))
            mm_tok[g] = tt
            P2S["q_last"] = tt

            def do_evac(g=g, oc=oc, cb=cb, tt=tt):
                ev_tok[g] = B.emit("scalar", lambda s:
                                   s.activation(QT[:, oc * NO + cb * 512: oc * NO + cb * 512 + 512],
                                                pb8[g % 8], AF.Identity),
                                   waits=[tt], inc=True)
                last_ev_bank[g % 8] = ev_tok[g]
            pend_evac.append(do_evac)

        def emit_k_group(oc, cb):
            g = P2S["g"]; P2S["g"] += 1
            kn = P2S["kn"]; P2S["kn"] += 1
            w0 = [t_xnt_cb[cb], t_wk, ev_tok.get(g - 8)]
            tt = None
            for c in range(8):
                tt = B.emit("tensor", lambda t, c=c, g=g, oc=oc, cb=cb:
                            t.matmul(pb8[g % 8],
                                     WLD[1][:, c * 1024 + oc * 128: c * 1024 + (oc + 1) * 128],
                                     XNT[:, c * NT + cb * 512: c * NT + cb * 512 + 512],
                                     start=(c == 0), stop=(c == 7)),
                            waits=(w0 if c == 0 else []), inc=(c == 7))
            mm_tok[g] = tt

            def do_evac(g=g, oc=oc, cb=cb, kn=kn, tt=tt):
                j = kn % 4
                ev_tok[g] = B.emit("scalar", lambda s:
                                   s.activation(KST[j][:, :], pb8[g % 8], AF.Identity),
                                   waits=[tt, kst_dma.get(kn - 4)], inc=True)
                last_ev_bank[g % 8] = ev_tok[g]
                kst_dma[kn] = B.dma("gpsimd", "d_ks%d" % (kn % 4),
                                    kt_d[oc * 128:(oc + 1) * 128, cb * 512:(cb + 1) * 512],
                                    KST[j][:, :], waits=[ev_tok[g]])
            pend_evac.append(do_evac)
        for i in range(NTILE + 1):
            if i < NTILE:
                t_ld_tok[i] = B.dma("sync", "d_x%d" % (i % 4), XT[i % 4][:, :],
                                    x_h[i * 128:(i + 1) * 128, :], waits=[xn_tok.get(i - 4)])
                emit_stats(i)
                emit_rstd(i)
            if i >= 1:
                emit_mrs(i - 1)
                emit_xn(i - 1)
            if i >= 4 and i % 4 == 0:
                cb = i // 4 - 1
                for fn in pend_evac:
                    fn()
                pend_evac.clear()
                tok = None
                for c in range(8):
                    tok = B.dma("sync", "d_xnt%d" % (cb % 2),
                                XNT[:, c * NT + cb * 512: c * NT + cb * 512 + 512],
                                xn_d[cb * 512:(cb + 1) * 512, c * 128:(c + 1) * 128],
                                waits=[store_tok[4 * cb + 3], store_tok[4 * cb + 2]],
                                transpose=True)
                t_xnt_cb[cb] = tok
                if cb < 4:
                    for oc in range(8):
                        emit_q_group(oc, cb)
            if i >= 6 and i % 4 == 2 and (i - 6) // 4 <= 6:
                for fn in pend_evac:
                    fn()
                pend_evac.clear()
                for oc in range(8):
                    emit_k_group(oc, (i - 6) // 4)
        for fn in pend_evac:
            fn()
        pend_evac.clear()
        for oc in range(8):
            emit_k_group(oc, 7)
        for fn in pend_evac:
            fn()
        pend_evac.clear()
        t_xnt = t_xnt_cb[7]

        t_wv = B.dma("gpsimd", "d_wv", WLD[0].rearrange("p (c j) -> p c j", j=1024), wld_ap(wv_h),
                     waits=[P2S["q_last"]])
        vst_dma = {}
        for ti in range(32):
            for heb in range(2):
                g = P2S["g"]; P2S["g"] += 1
                vg = ti * 2 + heb
                w0 = [t_xnt_cb[ti // 4], t_wv, ev_tok.get(g - 8)]
                tt = None
                for c in range(8):
                    tt = B.emit("tensor", lambda t, c=c, g=g, ti=ti, heb=heb:
                                t.matmul(pb8[g % 8],
                                         XNT[:, c * NT + ti * 128: c * NT + (ti + 1) * 128],
                                         WLD[0][:, c * 1024 + heb * 512: c * 1024 + (heb + 1) * 512],
                                         start=(c == 0), stop=(c == 7)),
                                waits=(w0 if c == 0 else []), inc=(c == 7))
                mm_tok[g] = tt
                j = vg % 4
                ev_tok[g] = B.emit("scalar", lambda s, g=g, j=j: s.activation(VST[j][:, :], pb8[g % 8], AF.Identity),
                                   waits=[tt, vst_dma.get(vg - 4)], inc=True)
                last_ev_bank[g % 8] = ev_tok[g]
                vst_dma[vg] = B.dma("gpsimd", "d_vs%d" % (vg % 4),
                                    v_d[ti, heb * 8:(heb + 1) * 8].rearrange("h p e -> p h e"),
                                    VST[j][:, :].rearrange("p (h e) -> p h e", e=64),
                                    waits=[ev_tok[g]])
        pe_p2_done = mm_tok[P2S["g"] - 1]
        pe_p2_cnt = B.cnt["tensor"]
        p2_stores = [(B.dma_sem(s), B.dcnt[s]) for s in
                     ("d_ks0", "d_ks1", "d_ks2", "d_ks3", "d_vs0", "d_vs1", "d_vs2", "d_vs3")]

        tva = None
        for VA in (VA1[0], VA1[1], VA2[0], VA2[1]):
            tva = B.emit("vector", lambda v, VA=VA: v.memset(
                VA.rearrange("p (c o) -> p c o", o=65)[:, :, 64:65], 1.0), waits=[pe_p2_done], inc=True)
        tva = B.emit("vector", lambda v: v.memset(ONE_A, 1.0), waits=[tva], inc=True)
        tva = B.emit("vector", lambda v: v.memset(ONE_B, 1.0), waits=[tva], inc=True)
        tva = B.emit("vector", lambda v: v.memset(ONE_C, 1.0), waits=[tva], inc=True)
        t_bias = B.dma("sync", "d_misc", BIASR, bias_h.rearrange("(o d) -> o d", o=1), waits=[tva])
        REC75 = STAT[:, 76:77]
        t_r75 = B.emit("vector", lambda v: v.memset(REC75, 7.5), waits=[], inc=True)
        t_r75 = B.emit("vector", lambda v: v.reciprocal(REC75, REC75), waits=[t_r75], inc=True)

        kp_tok = {}
        va_tok = {}
        for hp0 in (0, 1):
            w = p2_stores + [tva]
            kp_tok[hp0] = B.dma("sync", "d_kp%d" % hp0, KP[hp0][:, :], kt_d[hp0 * 128:(hp0 + 1) * 128, :],
                                waits=w)
            va_tok[hp0] = []
            for (vi, (VA, h)) in enumerate(((VA1[hp0], 2 * hp0), (VA2[hp0], 2 * hp0 + 1))):
                va_tok[hp0].append(B.dma("sync", "d_va%d_%d" % (vi, hp0),
                                         VA.rearrange("p (c o) -> p c o", o=65)[:, :, 0:64],
                                         v_d[:, h].rearrange("c p e -> p c e"), waits=w))

        SPp = [PSA[:, 0:1024], PSA[:, 1024:2048]]
        ASET = [(PSB[0:65, 0:512], PSB[0:65, 512:1024]),
                (PSB[0:65, 1024:1536], PSB[0:65, 1536:2048])]
        pe_hp_last = {}
        prev_exp_for_buf = [None, None]
        prev_av_for_E = [None, None, None, None]
        prev_set_tail = [None, None]
        z_prev = {1: None, 2: None}
        src_free = {}
        chainsrc = {}
        xnq_tok = {}
        chain_state = {"last": None, "prev_store": None}
        xd_store = []
        wd_store = []
        az_prev = {}
        qload_tok = {}

        def quant_chain_steps(src_ap, dst_dram, start_wait):
            toks = {}
            steps = []
            st = {"tv": None}
            w_first = [start_wait, chain_state["last"]]
            ps = chain_state["prev_store"]
            iscb = QIS.rearrange("p (b o) -> p b o", o=1).broadcast_to([128, 32, 32])
            scb = QSC.rearrange("p (b o) -> p b o", o=1).broadcast_to([128, 32, 32])

            def op(build, key=None):
                def run(build=build, key=key):
                    w = w_first if st["tv"] is None else [st["tv"]]
                    st["tv"] = B.emit("vector", build, waits=w, inc=True)
                    if key:
                        toks[key] = st["tv"]
                steps.append(run)

            op(lambda v, src_ap=src_ap: v.tensor_reduce(
                QBM, src_ap.rearrange("p (b w) -> p b w", w=32), AX.X, ALU.max,
                apply_absolute_value=True))
            op(lambda v: v.tensor_tensor(
                QSC.rearrange("p (b o) -> p b o", o=1).broadcast_to([128, 32, 1]),
                QBM.rearrange("p (b o) -> p b o", o=1).broadcast_to([128, 32, 1]),
                REC75.rearrange("p (b o) -> p b o", o=1).broadcast_to([128, 32, 1]),
                ALU.mult))
            op(lambda v: v.tensor_scalar(QSC, QSC, 1e-30, None, ALU.max))
            op(lambda v: v.reciprocal(QIS, QSC))
            op(lambda v, src_ap=src_ap, iscb=iscb: v.tensor_tensor(
                U_.rearrange("p (b w) -> p b w", w=32),
                src_ap.rearrange("p (b w) -> p b w", w=32), iscb, ALU.mult), key="src_free")
            op(lambda v: v.tensor_scalar(U_, U_, 7.5, -7.5, ALU.min, ALU.max))
            op(lambda v: v.tensor_scalar(PQ1, U_, -1.0, None, ALU.mult))
            op(lambda v: v.tensor_tensor(AU_, U_, PQ1, ALU.max))
            op(lambda v: v.tensor_scalar(PQ1, AU_, 2.0, -4.0, ALU.is_ge, ALU.mult))
            op(lambda v: v.tensor_scalar(PQ2, AU_, 4.0, -2.0, ALU.is_ge, ALU.mult))
            op(lambda v: v.scalar_tensor_tensor(ISTq, PQ1, 8.0, PQ2, ALU.add, ALU.add))
            op(lambda v: v.tensor_scalar(PQ1, AU_, 2.0, 0.125, ALU.is_ge, ALU.mult))
            op(lambda v: v.tensor_scalar(PQ2, AU_, 4.0, 0.25, ALU.is_ge, ALU.mult))
            op(lambda v: v.scalar_tensor_tensor(STPq, PQ1, 0.125, PQ2, ALU.add, ALU.add))
            op(lambda v: v.tensor_tensor(TQ, AU_, ISTq, ALU.mult))
            op(lambda v: v.tensor_scalar(X5, TQ, 0.5, None, ALU.add))
            for wgt in (16.0, 8.0, 4.0, 2.0, 1.0):
                op(lambda v, wgt=wgt: v.tensor_scalar(PQ1, X5, wgt, -wgt, ALU.is_ge, ALU.mult))
                op(lambda v: v.tensor_tensor(X5, X5, PQ1, ALU.add))
            op(lambda v: v.tensor_scalar(RQ, TQ, 0.5, None, ALU.add))
            op(lambda v: v.tensor_tensor(RQ, RQ, X5, ALU.subtract))
            op(lambda v: v.tensor_tensor(RQ, RQ, STPq, ALU.mult))
            op(lambda v: v.tensor_scalar(PQ1, U_, 0.0, 2.0, ALU.is_ge, ALU.mult))
            op(lambda v: v.tensor_tensor(PQ2, RQ, PQ1, ALU.mult))
            op(lambda v: v.tensor_tensor(RQ, PQ2, RQ, ALU.subtract))

            def run_final():
                tv = B.emit("vector", lambda v, scb=scb: v.tensor_tensor(
                    XDBF.rearrange("p (b w) -> p b w", w=32),
                    RQ.rearrange("p (b w) -> p b w", w=32), scb, ALU.mult),
                    waits=[st["tv"], ps], inc=True)
                toks["last"] = tv
                chain_state["last"] = tv
                toks["store"] = B.dma("gpsimd", "d_xd", dst_dram, XDBF, waits=[tv])
                chain_state["prev_store"] = toks["store"]
            steps.append(run_final)
            return toks, steps

        def task_load(u):
            if u < 16:
                qload_tok[u] = B.dma("sync", "d_xq%d" % (u % 2), XQ[u % 2][:, :],
                                     x_h[u * 128:(u + 1) * 128, :], waits=[src_free.get(u - 2)])
            elif u < 24:
                j = u - 16
                qload_tok[u] = B.dma("sync", "d_xq%d" % (u % 2), XQ[u % 2][:, :],
                                     wf_h[j * 128:(j + 1) * 128, :], waits=[src_free.get(u - 2)])

        task_load(0)
        for u in range(32):
            hp, qgi = divmod(u, 4)
            h1, h2 = 2 * hp, 2 * hp + 1
            bufK = hp % 2
            aset = u % 2
            A1, A2 = ASET[aset]
            if qgi == 0 and 1 <= hp <= 6:
                nb = (hp + 1) % 2
                w = [pe_hp_last[hp - 1]]
                kp_tok[hp + 1] = B.dma("sync", "d_kp%d" % nb, KP[nb][:, :],
                                       kt_d[(hp + 1) * 128:(hp + 2) * 128, :], waits=w)
                va_tok[hp + 1] = []
                for (vi, (VA, h)) in enumerate(((VA1[nb], 2 * (hp + 1)), (VA2[nb], 2 * (hp + 1) + 1))):
                    va_tok[hp + 1].append(B.dma("sync", "d_va%d_%d" % (vi, nb),
                                                VA.rearrange("p (c o) -> p c o", o=65)[:, :, 0:64],
                                                v_d[:, h].rearrange("c p e -> p c e"), waits=w))
            unit_ready = [kp_tok[hp]] + va_tok[hp]
            task_load(u + 1)

            if u < 16:
                t_xnq = B.emit("scalar", lambda s, u=u: s.activation(
                    XNQR[u % 2][:, :], XQ[u % 2][:, :], AF.Identity,
                    scale=STAT[:, u:u + 1], bias=STAT[:, 32 + u:33 + u]),
                    waits=[qload_tok[u], chainsrc.get(u - 2)], inc=True)
                xnq_tok[u] = t_xnq
            chain_steps = None
            ctoks = None
            if u < 16:
                ctoks, chain_steps = quant_chain_steps(
                    XNQR[u % 2][:, :], xd_d[u * 128:(u + 1) * 128, :], xnq_tok[u])
                src_free[u] = xnq_tok[u]
            elif u < 24:
                j2 = u - 16
                ctoks, chain_steps = quant_chain_steps(
                    XQ[u % 2][:, :], wd_d[j2 * 128:(j2 + 1) * 128, :], qload_tok[u])
            if u == 13:
                p3_wp = B.dma("gpsimd", "d_wp", WPB.rearrange("p (c j) -> p c j", j=1024),
                              wld_ap(wp_h), waits=[(B.sem["tensor"], pe_p2_cnt)])
            if u == 17:
                p3_xdt = None
                for ic in range(8):
                    p3_xdt = B.dma("sync", "d_xdt2", XDT[:, ic * NO:(ic + 1) * NO],
                                   xd_d[:, ic * 128:(ic + 1) * 128],
                                   waits=[xd_store[-1]], transpose=True)
            if u == 25:
                p3_wdt = None
                for ic in range(8):
                    p3_wdt = B.dma("sync", "d_wdt2", WDT[:, ic * D:(ic + 1) * D],
                                   wd_d[:, ic * 128:(ic + 1) * 128],
                                   waits=[wd_store[-1]], transpose=True)

            qcol = hp * NO + qgi * 512
            exp_tok = {}
            sc_tok = {}
            av_list = {}

            def do_av(t2, A1=A1, A2=A2, bufK=bufK, aset=aset):
                wav = [exp_tok[t2]]
                if t2 == 0:
                    wav.append(prev_set_tail[aset])
                    if u == 0:
                        wav += [last_ev_bank.get(4), last_ev_bank.get(5)]
                B.emit("tensor", lambda t, t2=t2, A1=A1, bufK=bufK: t.matmul(
                    A1, VA1[bufK][:, t2 * 65:t2 * 65 + 65],
                    EB4[t2 % 4][:, 0:512],
                    start=(t2 == 0), stop=(t2 == 31)), waits=wav, inc=False)
                av_list[t2] = B.emit("tensor", lambda t, t2=t2, A2=A2, bufK=bufK: t.matmul(
                    A2, VA2[bufK][:, t2 * 65:t2 * 65 + 65],
                    EB4[t2 % 4][:, 512:1024],
                    start=(t2 == 0), stop=(t2 == 31)), waits=[], inc=True)

            for tc in range(32):
                b = tc % 2
                if tc < 2:
                    w_sc = [prev_exp_for_buf[b]]
                    if tc == 0:
                        w_sc += unit_ready
                    if u == 0:
                        w_sc += [last_ev_bank.get(2 * b), last_ev_bank.get(2 * b + 1)]
                else:
                    w_sc = [exp_tok[tc - 2]]
                B.emit("tensor", lambda t, tc=tc, b=b, bufK=bufK, qcol=qcol: t.matmul(
                    SPp[b][:, 0:512],
                    KP[bufK][0:64, tc * 128:(tc + 1) * 128],
                    QT[0:64, qcol:qcol + 512],
                    start=True, stop=True, tile_position=(0, 0)), waits=w_sc, inc=False)
                sc_tok[tc] = B.emit("tensor", lambda t, tc=tc, b=b, bufK=bufK, qcol=qcol: t.matmul(
                    SPp[b][:, 512:1024],
                    KP[bufK][64:128, tc * 128:(tc + 1) * 128],
                    QT[64:128, qcol:qcol + 512],
                    start=True, stop=True, tile_position=(64, 0)), waits=[], inc=True)
                we = [sc_tok[tc]]
                if tc >= 4:
                    we.append(av_list[tc - 4])
                else:
                    we.append(prev_av_for_E[tc % 4])
                exp_tok[tc] = B.emit("scalar", lambda s, b=b, tc=tc: s.activation(
                    EB4[tc % 4][:, 0:1024], SPp[b][:, :], AF.Exp, scale=0.125), waits=we, inc=True)
                if tc >= 2:
                    do_av(tc - 2)
                if chain_steps and tc < 27:
                    chain_steps.pop(0)()
            do_av(30)
            do_av(31)
            prev_exp_for_buf = [exp_tok[30], exp_tok[31]]
            prev_av_for_E = [av_list[28], av_list[29], av_list[30], av_list[31]]
            if qgi == 3:
                pe_hp_last[hp] = av_list[31]

            tva_ = B.emit("vector", lambda v, aset=aset: v.tensor_copy(
                ASF2[:, :], PSB[0:64, aset * 1024:aset * 1024 + 1024]),
                waits=[av_list[31], az_prev.get(1), az_prev.get(2)], inc=True)
            tz = {}
            for hs, Abank, hh in ((1, A1, h1), (2, A2, h2)):
                tz[hs] = B.emit("vector", lambda v, hs=hs, Abank=Abank: v.tensor_copy(
                    ZR[hs], Abank[64:65, :]), waits=[tva_ if hs == 1 else tz[1], z_prev.get(hs)], inc=True)
                az_prev[hs] = B.dma("gpsimd", "d_az%d" % hs,
                                    a_d[qgi * 4:(qgi + 1) * 4, hh].rearrange("b p j -> p b j"),
                                    ASF2[:, (hs - 1) * 512:hs * 512].rearrange("p (b j) -> p b j", j=128),
                                    waits=[tva_])
                z_prev[hs] = B.dma("gpsimd", "d_z%d" % hs,
                                   z_d[hh:hh + 1, qgi * 512:(qgi + 1) * 512], ZR[hs], waits=[tz[hs]])
            prev_set_tail[aset] = tz[2]
            while chain_steps:
                chain_steps.pop(0)()

            if u < 16:
                chainsrc[u] = ctoks["src_free"]
                xd_store.append(ctoks["store"])
            elif u < 24:
                src_free[u] = ctoks["src_free"]
                wd_store.append(ctoks["store"])
        last_chain = chain_state["last"]

        t_wp = p3_wp
        ZV = afp(24, 128, 256)
        ZRV = afp(25, 128, 256)
        t_zv = B.dma("sync", "d_zv", ZV, z_d.rearrange("h (a q) -> (h a) q", a=8),
                     waits=[z_prev[1], z_prev[2], last_chain])
        t_zr = B.emit("vector", lambda v: v.reciprocal(ZRV, ZV), waits=[t_zv], inc=True)
        t_zrs = B.dma("gpsimd", "d_zr", zr_d.rearrange("h (a q) -> (h a) q", a=8), ZRV, waits=[t_zr])
        t_xdt = p3_xdt
        t_wdt = p3_wdt

        PPq = [[PSA[:, 0:512], PSA[:, 512:1024]], [PSA[:, 1024:1536], PSA[:, 1536:2048]]]
        PFq = PSB[:, 0:1024]
        DP = PSB[:, 1024:2048]
        ZPK2 = [AFP[0:2, 13312:14336], AFP[0:2, 14464:15488]]
        ONESEL = AFP[0:2, 12288:12416]
        ROWB = AFP[0:1, 12416:12544]
        t_osel = B.emit("vector", lambda v: v.memset(ONESEL[0:1, 0:64], 1.0),
                        waits=[t_zrs, last_chain], inc=True)
        t_osel = B.emit("vector", lambda v: v.memset(ONESEL[0:1, 64:128], 0.0),
                        waits=[t_osel], inc=True)
        t_osel = B.emit("vector", lambda v: v.memset(ROWB[0:1, 0:64], 0.0),
                        waits=[t_osel], inc=True)
        t_osel = B.emit("vector", lambda v: v.memset(ROWB[0:1, 64:128], 1.0),
                        waits=[t_osel], inc=True)
        t_osel = B.dma("sync", "d_os", ONESEL[1:2, :], ROWB[0:1, :], waits=[t_osel])
        dmm_tok = {}
        dmul_tok = {}
        xn4_tok = {}
        pmm_tok = {}
        sadd_tok = {}
        gelu_tok = {}
        ffn_tok = {}
        silu_tok = {}
        sg_tok = {}
        outtok = {}
        out_store = {}
        xb_tok = {}

        def p4_stage1(qt):
            pr = qt % 2
            w_z = [dmm_tok.get(qt - 2),
                   t_zrs if qt <= 1 else None,
                   az_prev[1] if qt <= 1 else None,
                   az_prev[2] if qt <= 1 else None]
            tz = None
            for pg in range(2):
                dst = ZPK2[pr][pg:pg + 1, :].rearrange("o (c j) -> o c j", j=128)
                srcz = zr_d[pg::2, qt * 128:(qt + 1) * 128].rearrange("(o c) j -> o c j", o=1)
                tz = B.dma("sync", "d_dt%d" % pr, dst, srcz, waits=(w_z if pg == 0 else []))
            t_x4 = B.dma("sync", "d_p4x%d" % pr, XA[pr][:, :], x_h[qt * 128:(qt + 1) * 128, :],
                         waits=[xn4_tok.get(qt - 2),
                                last_chain if qt <= 1 else None,
                                xnq_tok[15] if qt <= 1 else None])
            w_at = [dmul_tok.get(qt - 2), t_wp,
                    t_zrs if qt <= 1 else None,
                    wd_store[-1] if qt == 0 else None,
                    az_prev[1] if qt == 0 else None,
                    az_prev[2] if qt == 0 else None,
                    pe_hp_last[7] if qt == 0 else None]
            t_at_e = B.dma("sync", "d_p4a%d" % pr,
                           ATB[pr][0:64, :].rearrange("p (c j) -> p c j", j=128),
                           a_d[qt].rearrange("(c t) p j -> t c p j", t=2)[0].rearrange("c p j -> p c j"),
                           waits=w_at)
            t_at_o = B.dma("sync", "d_p4a%d" % pr,
                           ATB[pr][64:128, :].rearrange("p (c j) -> p c j", j=128),
                           a_d[qt].rearrange("(c t) p j -> t c p j", t=2)[1].rearrange("c p j -> p c j"),
                           waits=[])
            if qt >= 1:
                pr2 = (qt - 1) % 2
                xb_tok[qt - 1] = B.dma("sync", "d_p4b%d" % pr2, XB[pr2][:, :],
                                       x_h[(qt - 1) * 128:qt * 128, :],
                                       waits=[outtok.get(qt - 3),
                                              last_chain if qt - 1 <= 1 else None])
            w_dm = [tz, dmul_tok.get(qt - 1), t_osel if qt <= 1 else None,
                    prev_set_tail[1] if qt == 0 else None]
            tmm = None
            for nh in range(2):
                tmm = B.emit("tensor", lambda t, nh=nh, pr=pr:
                             t.matmul(DP[:, nh * 512:(nh + 1) * 512],
                                      ONESEL[:, :], ZPK2[pr][:, nh * 512:(nh + 1) * 512],
                                      start=True, stop=True),
                             waits=(w_dm if nh == 0 else []), inc=(nh == 1))
            dmm_tok[qt] = tmm
            t_xn4 = B.emit("scalar", lambda s, qt=qt, pr=pr: s.activation(
                XNQ4[qt % 2][:, :], XA[pr][:, :], AF.Identity,
                scale=STAT[:, qt:qt + 1], bias=STAT[:, 32 + qt:33 + qt]),
                waits=[t_x4, sadd_tok.get(qt - 2),
                       az_prev[1] if qt <= 1 else None,
                       az_prev[2] if qt <= 1 else None], inc=True)
            xn4_tok[qt] = t_xn4
            t_dm = B.emit("vector", lambda v, pr=pr: v.tensor_tensor(
                DTp[pr][:, :], ATB[pr][:, :], DP[:, :], ALU.mult),
                waits=[t_at_o, dmm_tok[qt], gelu_tok.get(qt - 2)], inc=True)
            dmul_tok[qt] = t_dm

        def p4_stage2(qt):
            pr = qt % 2
            t_gelu = B.emit("scalar", lambda s, qt=qt, pr=pr: s.activation(
                ATT[pr][:, :], DTp[pr][:, :], AF.Gelu),
                waits=[dmul_tok[qt], pmm_tok.get(qt - 2)], inc=True)
            gelu_tok[qt] = t_gelu
            ts_list = []
            for half in range(2):
                w0 = [t_gelu, sadd_tok.get(qt - 2)]
                tt = None
                for hc in range(8):
                    tt = B.emit("tensor", lambda t, hc=hc, half=half, qt=qt, pr=pr:
                                t.matmul(PPq[pr][half], ATT[pr][:, hc * 128:(hc + 1) * 128],
                                         WPB[:, hc * 1024 + half * 512: hc * 1024 + half * 512 + 512],
                                         start=(hc == 0), stop=(hc == 7)),
                                waits=(w0 if hc == 0 else []), inc=(hc == 7))
                if half == 1:
                    pmm_tok[qt] = tt
                ts_ = B.emit("vector", lambda v, half=half, pr=pr, qt=qt: v.tensor_tensor(
                    STLp[pr][:, half * 512:(half + 1) * 512], PPq[pr][half],
                    XNQ4[qt % 2][:, half * 512:(half + 1) * 512],
                    ALU.add), waits=[tt, xn4_tok[qt], sg_tok.get(qt - 2)], inc=True)
                ts_list.append(ts_)
            sadd_tok[qt] = ts_list[-1]

        def emit_ffn(qt):
            w0 = [t_xdt, t_wdt, silu_tok.get(qt - 1)]
            if qt == 0:
                w0.append(prev_set_tail[0])
                w0.append(t_bias)
            tt = None
            for half in range(2):
                for ic in range(8):
                    tt = B.emit("tensor", lambda t, ic=ic, half=half, qt=qt:
                                t.matmul(PFq[:, half * 512:(half + 1) * 512],
                                         XDT[:, ic * NO + qt * 128: ic * NO + (qt + 1) * 128],
                                         WDT[:, ic * 1024 + half * 512: ic * 1024 + half * 512 + 512],
                                         start=(ic == 0), stop=False),
                                waits=(w0 if (half == 0 and ic == 0) else []), inc=False)
                tt = B.emit("tensor", lambda t, half=half:
                            t.matmul(PFq[:, half * 512:(half + 1) * 512],
                                     ONE_C[:, :], BIASR[:, half * 512:half * 512 + 512],
                                     start=False, stop=True),
                            waits=[], inc=(half == 1))
            ffn_tok[qt] = tt

        def p4_stage3(qt):
            pr = qt % 2
            t_silu = B.emit("scalar", lambda s, pr=pr: s.activation(GSp[pr][:, :], PFq[:, :], AF.Silu),
                            waits=[ffn_tok[qt], out_store.get(qt - 2)], inc=True)
            silu_tok[qt] = t_silu
            if qt + 1 < OTILE:
                emit_ffn(qt + 1)
            tv = B.emit("vector", lambda v, pr=pr: v.tensor_tensor(
                GSp[pr][:, :], GSp[pr][:, :], STLp[pr][:, :], ALU.mult),
                waits=[t_silu, sadd_tok[qt]], inc=True)
            sg_tok[qt] = tv
            tv = B.emit("vector", lambda v, qt=qt, pr=pr: v.tensor_tensor(
                GSp[pr][:, :], GSp[pr][:, :], XB[pr][:, :], ALU.add),
                waits=[tv, xb_tok[qt]], inc=True)
            outtok[qt] = tv
            out_store[qt] = B.dma("gpsimd", "d_out%d" % pr, out_h[qt * 128:(qt + 1) * 128, :],
                                  GSp[pr][:, :], waits=[tv])

        p4_stage1(0)
        p4_stage1(1)
        emit_ffn(0)
        for qt in range(OTILE):
            p4_stage2(qt)
            if qt >= 1:
                p4_stage3(qt - 1)
            if qt + 2 < OTILE:
                p4_stage1(qt + 2)
        qlast = OTILE - 1
        xb_tok[qlast] = B.dma("sync", "d_p4b%d" % (qlast % 2), XB[qlast % 2][:, :],
                              x_h[qlast * 128:(qlast + 1) * 128, :],
                              waits=[outtok.get(qlast - 2)])
        p4_stage3(qlast)
        B.wait("gpsimd", [out_store[OTILE - 1], out_store[OTILE - 2]])
        B.wait("sync", [out_store[OTILE - 1], out_store[OTILE - 2]])

        @blk.sync
        def _(sync):
            for fn in B.ops["sync"]:
                fn(sync)

        @blk.gpsimd
        def _(g):
            for fn in B.ops["gpsimd"]:
                fn(g)

        @blk.vector
        def _(v):
            for fn in B.ops["vector"]:
                fn(v)

        @blk.scalar
        def _(s):
            for fn in B.ops["scalar"]:
                fn(s)

        @blk.tensor
        def _(t):
            for fn in B.ops["tensor"]:
                fn(t)

    return nc


_NC_CACHE = []


def kernel(L0_attn_X, L0_attn_Wq, L0_attn_Wk, L0_attn_Wv, L0_attn_Wpost,
           L0_ffn_weight, L0_ffn_bias):
    x = np.ascontiguousarray(np.asarray(L0_attn_X, dtype=np.float32))
    wq = np.ascontiguousarray(np.asarray(L0_attn_Wq, dtype=np.float32))
    wk = np.ascontiguousarray(np.asarray(L0_attn_Wk, dtype=np.float32))
    wv = np.ascontiguousarray(np.asarray(L0_attn_Wv, dtype=np.float32))
    wp = np.ascontiguousarray(np.asarray(L0_attn_Wpost, dtype=np.float32))
    wf = np.ascontiguousarray(np.asarray(L0_ffn_weight, dtype=np.float32))
    bias = np.ascontiguousarray(np.asarray(L0_ffn_bias, dtype=np.float32))

    if not _NC_CACHE:
        _NC_CACHE.append(build_program())
    nc = _NC_CACHE[0]

    in_maps = []
    for c in range(8):
        b, h = divmod(c, 2)
        own = x[b, h * 2048:(h + 1) * 2048]
        oth = x[b, (1 - h) * 2048:(2 - h) * 2048]
        xc = np.ascontiguousarray(np.concatenate([own, oth], axis=0))
        in_maps.append({"x": xc, "wq": wq, "wk": wk, "wv": wv,
                        "wpost": wp, "w": wf, "bias": bias})
    res = run_bass_kernel_spmd(nc, in_maps, core_ids=list(range(8)))
    out = np.empty((4, 4096, 1024), dtype=np.float32)
    for c in range(8):
        b, h = divmod(c, 2)
        out[b, h * 2048:(h + 1) * 2048] = res.results[c]["out"]
    return out
